# version 21
# baseline (speedup 1.0000x reference)
import numpy as np
import ml_dtypes
from contextlib import ExitStack

import concourse.bass as bass
import concourse.mybir as mybir
from concourse.bass_utils import run_bass_kernel_spmd

F32 = mybir.dt.float32
BF16 = mybir.dt.bfloat16
AF = mybir.ActivationFunctionType
ALU = mybir.AluOpType
AX = mybir.AxisListType
BF = ml_dtypes.bfloat16

NCORES = 8
D = 2048
NH = 32
DS = 64
CONV_DIM = 6144
NOC = 65
E = 8
FC = 11
FSC = 22
HC = 16
CH = 128
NEG = -1.0e30
LNEPS = 1e-5


def _nch(total, m=512):
    out, s = [], 0
    while s < total:
        out.append((s, min(m, total - s)))
        s += m
    return out


class Sem:
    def __init__(self, ctx, nc, name):
        self.h = ctx.enter_context(nc.semaphore(name))
        self.n = 0

    def inc(self, inst, k=1):
        inst.then_inc(self.h, k)
        self.n += k
        return inst

    def dinc(self, inst):
        return self.inc(inst, 16)

    def wait(self, eng, val=None):
        eng.wait_ge(self.h, self.n if val is None else val)


def build(T, n_cores, dbg=False):
    NC = T // CH
    TH = T + 3
    nc = bass.Bass("TRN2", target_bir_lowering=False, num_devices=n_cores)

    din = lambda n, s: nc.dram_tensor(n, list(s), F32, kind="ExternalInput").ap()
    i_ufm = din("u_fm", (HC, 128, TH))
    i_utm = din("u_tm", (T, D))
    i_wint = din("w_in_t", (NOC, 128, HC, 128))
    i_convw = din("conv_w_c", (128, 48 * 4))
    i_convb = din("conv_b_c", (128, 48))
    i_dcol = din("d_col", (128, NH))
    i_zb = din("zb_c", (128, 16))
    i_wot = din("w_out_t", (HC, 128, D))
    i_gwt = din("gw_t", (HC, 128, E))
    i_wg = din("w_gate_t", (E, FC, 128, HC, 128))
    i_wu = din("w_up_t", (E, FC, 128, HC, 128))
    i_wd = din("w_down_t", (E, FC, 128, D))
    i_swg = din("sw_gate_t", (FSC, 128, HC, 128))
    i_swu = din("sw_up_t", (FSC, 128, HC, 128))
    i_swd = din("sw_down_t", (FSC, 128, D))
    i_pmask = din("pmask", (128, NCORES))
    o_out = nc.dram_tensor("out", [T, D], F32, kind="ExternalOutput").ap()

    sk = "ExternalOutput" if dbg else "Internal"
    d_xbc = nc.dram_tensor("xbc_sp", [NC, 128, 48, CH], BF16, kind=sk).ap()
    d_xdtT = nc.dram_tensor("xdtT_sp", [NC, 128, D], BF16, kind=sk).ap()
    d_pvl = nc.dram_tensor("prevloc_sp", [NC, 128, 1024], F32, kind=sk).ap()
    d_g = nc.dram_tensor("g_sp", [2, NC, 128, FC, CH], BF16, kind="Internal").ap()
    d_yz = nc.dram_tensor("yz_sp", [NC, 128, D], BF16, kind=sk).ap()
    d_ccin = nc.dram_tensor("cc_in", [128, 1040], F32, kind="Internal").ap()
    d_ccout = nc.dram_tensor("cc_out", [128 * n_cores, 1040], F32, kind="Internal").ap()
    d_wrm_in = nc.dram_tensor("wrm_in", [1, 8], F32, kind="Internal").ap()
    d_wrm_out = nc.dram_tensor("wrm_out", [n_cores, 8], F32, kind="Internal").ap()
    dbg_t = {}
    if dbg:
        for nm, sh in [("h1_dbg", (128, NC * D)), ("cw_dbg", (128, NC * E)),
                       ("siluz_dbg", (128, 16 * T)), ("ndt_dbg", (32, T)),
                       ("pref_dbg", (128, 1024)),
                       ("r_dbg", (NC, 128, D))]:
            dbg_t[nm] = nc.dram_tensor(nm, list(sh),
                                       BF16 if nm in ("siluz_dbg",) else F32,
                                       kind="ExternalOutput").ap()

    c_identbf = nc.inline_tensor(np.eye(128, dtype=np.float32).astype(BF), "c_identbf").ap()
    c_identf = nc.inline_tensor(np.eye(128, dtype=np.float32), "c_identf").ap()
    maskt_np = np.where(np.arange(128)[None, :] <= np.arange(128)[:, None], 0.0, NEG)
    c_maskt = nc.inline_tensor(maskt_np.astype(np.float32), "c_maskt").ap()
    c_maskt2 = nc.inline_tensor(
        np.hstack([maskt_np, maskt_np]).astype(np.float32), "c_maskt2").ap()
    c_selcd = nc.inline_tensor(
        (np.arange(NH)[:, None] // 2 == np.arange(16)[None, :]).astype(np.float32),
        "c_selcd").ap()
    c_halfsel = nc.inline_tensor(
        (np.arange(128)[None, :] // 64 == np.arange(NH)[:, None] % 2).astype(np.float32),
        "c_halfsel").ap()
    c_cst = nc.inline_tensor(
        np.tile(np.array([[LNEPS, -0.5, -1.0, 0.0]], np.float32), (128, 1)),
        "c_cst").ap()

    ctx = ExitStack()
    with ctx:
        en = ctx.enter_context
        sb = lambda n, s, d_: en(nc.sbuf_tensor(n, list(s), d_))

        y_acc = en(nc.sbuf_tensor("y_acc", [128, NC * D], F32, side="right"))
        h1T = en(nc.sbuf_tensor("h1T", [128, HC * T], BF16, side="right"))
        cw_s = sb("cw_s", (128, NC * E), F32)
        ndt = sb("ndt", (32, T), F32)
        convw_s = sb("convw_s", (128, 48 * 4), F32)
        convb_s = sb("convb_s", (128, 48), F32)
        dcol_s = sb("dcol_s", (128, NH), F32)
        zb_s = sb("zb_s", (128, 16), F32)
        gw_s = sb("gw_s", (128, HC * E), BF16)
        identbf_s = sb("identbf_s", (128, 128), BF16)
        identf_s = sb("identf_s", (128, 128), F32)
        maskt_s = sb("maskt_s", (128, 128), F32)
        maskt2_s = sb("maskt2_s", (128, 256), F32)
        selcd_s = sb("selcd_s", (32, 16), F32)
        halfsel_s = sb("halfsel_s", (32, 128), F32)
        zeros32 = sb("zeros32", (32, 128), F32)
        pmask_s = sb("pmask_s", (128, NCORES), F32)
        cst_s = sb("cst_s", (128, 4), F32)

        sem = lambda n: Sem(ctx, nc, n)
        s_cl = sem("s_cl")
        s_clg = sem("s_clg")
        s_ini = sem("s_ini")
        s_ccw = sem("s_ccw")
        block = en(nc.Block())

        @block.sync
        def _(sync):
            for dst, src in [(convw_s, i_convw), (convb_s, i_convb),
                             (dcol_s, i_dcol), (zb_s, i_zb),
                             (identf_s, c_identf), (maskt_s, c_maskt),
                             (maskt2_s, c_maskt2),
                             (selcd_s, c_selcd), (halfsel_s, c_halfsel),
                             (pmask_s, i_pmask), (cst_s, c_cst)]:
                s_cl.dinc(sync.dma_start(out=dst[:], in_=src[:]))

        @block.gpsimd
        def _(g):
            s_clg.dinc(g.dma_start(out=identbf_s[:], in_=c_identbf[:]))
            s_clg.dinc(g.dma_start(
                out=gw_s[:].rearrange("p (hc e) -> p hc e", hc=HC, e=E),
                in_=i_gwt[:].transpose([1, 0, 2])))
            s_ccw.inc(g.collective_compute(
                "AllGather", ALU.bypass,
                ins=[d_wrm_in[:]], outs=[d_wrm_out[:]],
                replica_groups=[list(range(n_cores))]))

        @block.vector
        def _(v):
            s_ini.inc(v.memset(zeros32[:], 0.0))

        ssd_scope = ExitStack()
        es = ssd_scope.enter_context
        sbs = lambda n, s, d_: es(nc.sbuf_tensor(n, list(s), d_))
        siluz = None

        with ssd_scope:
            sz_s = sbs("sz_s", (128, 16 * T), BF16)
            prefix_s = sbs("prefix_s", (128, 1024), F32)
            cdpre_s = sbs("cdpre_s", (128, 16 * NC), F32)
            cdp_s = sbs("cdp_s", (128, 16), F32)
            prev_ab = sbs("prev_ab", (128, 2 * 1024), F32)

            @block.vector
            def _(v):
                s_ini.inc(v.memset(prefix_s[:], 0.0))
                s_ini.inc(v.memset(prev_ab[:], 0.0))
                s_ini.inc(v.memset(cdp_s[:], 1.0))
            INI_N = s_ini.n

            @block.tensor
            def _(t):
                s_cl.wait(t); s_clg.wait(t); s_ini.wait(t, INI_N)

            @block.vector
            def _(v):
                s_cl.wait(v); s_clg.wait(v)

            @block.scalar
            def _(s):
                s_cl.wait(s); s_clg.wait(s); s_ini.wait(s, INI_N)

            @block.gpsimd
            def _(g):
                s_cl.wait(g); s_ini.wait(g, INI_N)

            @block.sync
            def _(sync):
                s_cl.wait(sync); s_clg.wait(sync); s_ini.wait(sync, INI_N)

            with ExitStack() as p1:
                e1 = p1.enter_context
                sb1 = lambda n, s, d_: e1(nc.sbuf_tensor(n, list(s), d_))
                u_bf = sb1("u_bf", (128, HC * TH), BF16)
                wbuf = sb1("wbuf", (128, 2 * HC * 128), BF16)
                evbuf = sb1("evbuf", (128, 2 * TH), F32)
                cvt0 = sb1("cvt0", (128, T), F32)
                cvt1 = sb1("cvt1", (128, T), F32)
                cvout = sb1("cvout", (128, T), BF16)
                ps_x = [e1(nc.psum_tensor(f"ps_x{i}", [128, 1056], F32))
                        for i in range(2)]

                s_wl2 = [sem("s_wlA"), sem("s_wlB")]
                s_ul = sem("s_ul")
                s_mm1 = sem("s_mm1")
                s_ev1 = sem("s_ev1")
                s_cv = sem("s_cv")
                s_sp1 = sem("s_sp1")

                @block.gpsimd
                def _(g):
                    for dc in range(HC):
                        s_ul.dinc(g.dma_start(
                            out=u_bf[:, dc * TH:(dc + 1) * TH], in_=i_ufm[dc]))
                    for oc in (0, 1):
                        s_wl2[oc % 2].dinc(g.dma_start(
                            out=wbuf[:, (oc % 2) * HC * 128:(oc % 2 + 1) * HC * 128]
                            .rearrange("p (dc o) -> p dc o", dc=HC, o=128),
                            in_=i_wint[oc]))

                for oc in range(NOC):
                    n_tok = TH if oc < 48 else T
                    wl_need = 16 * (oc // 2 + 1)
                    ev_need = max(0, oc - 1)

                    @block.tensor
                    def _(t, oc=oc, n_tok=n_tok, wl_need=wl_need, ev_need=ev_need):
                        t.wait_ge(s_wl2[oc % 2].h, wl_need)
                        t.wait_ge(s_ev1.h, ev_need)
                        if oc == 0:
                            t.wait_ge(s_ul.h, 16 * HC)
                        wb = wbuf[:, (oc % 2) * HC * 128:]
                        ps = ps_x[oc % 2]
                        off = 0 if oc < 48 else 3
                        ins = None
                        for (ns, nw) in _nch(n_tok):
                            for dc in range(HC):
                                ins = t.matmul(
                                    ps[:, ns:ns + nw],
                                    wb[:, dc * 128:(dc + 1) * 128],
                                    u_bf[:, dc * TH + off + ns:dc * TH + off + ns + nw],
                                    start=(dc == 0), stop=(dc == HC - 1))
                        s_mm1.inc(ins)

                    if oc + 2 < NOC:
                        @block.gpsimd
                        def _(g, oc=oc):
                            s_mm1.wait(g)
                            s_wl2[oc % 2].dinc(g.dma_start(
                                out=wbuf[:, (oc % 2) * HC * 128:(oc % 2 + 1) * HC * 128]
                                .rearrange("p (dc o) -> p dc o", dc=HC, o=128),
                                in_=i_wint[oc + 2]))

                    @block.scalar
                    def _(s, oc=oc, n_tok=n_tok):
                        s_mm1.wait(s)
                        ps = ps_x[oc % 2]
                        if oc < 48:
                            s_cv.wait(s)
                            s_cl.wait(s)
                            s_ev1.inc(s.copy(
                                out=evbuf[:, (oc % 2) * TH:(oc % 2) * TH + n_tok],
                                in_=ps[:, 0:n_tok]))
                        elif oc < 64:
                            zi = oc - 48
                            s_ev1.inc(s.activation(
                                out=sz_s[:, zi * T:(zi + 1) * T],
                                in_=ps[:, 0:n_tok], func=AF.Silu,
                                bias=zb_s[:, zi:zi + 1]))
                        else:
                            s_ev1.inc(s.activation(
                                out=ndt[:, 0:T], in_=ps[0:32, 0:n_tok], func=AF.Exp))
                            s.wait_ge(s_ev1.h, s_ev1.n)
                            s_ev1.inc(s.activation(
                                out=ndt[:, 0:T], in_=ndt[:, 0:T],
                                func=AF.Ln, bias=1.0))

                    if oc < 48:
                        @block.vector
                        def _(v, oc=oc):
                            s_ev1.wait(v)
                            s_sp1.wait(v)
                            ev = evbuf[:, (oc % 2) * TH:]
                            w = lambda k: convw_s[:, oc * 4 + k:oc * 4 + k + 1]
                            s_cv.inc(v.tensor_scalar(
                                cvt0[:], ev[:, 0:T], w(0), convb_s[:, oc:oc + 1],
                                ALU.mult, ALU.add))
                            s_cv.wait(v)
                            s_cv.inc(v.scalar_tensor_tensor(
                                cvt1[:], ev[:, 1:T + 1], w(1), cvt0[:], ALU.mult, ALU.add))
                            s_cv.wait(v)
                            s_cv.inc(v.scalar_tensor_tensor(
                                cvt0[:], ev[:, 2:T + 2], w(2), cvt1[:], ALU.mult, ALU.add))
                            s_cv.wait(v)
                            s_cv.inc(v.scalar_tensor_tensor(
                                cvout[:], ev[:, 3:T + 3], w(3), cvt0[:], ALU.mult, ALU.add))

                        @block.sync
                        def _(sync, oc=oc):
                            s_cv.wait(sync)
                            s_sp1.dinc(sync.dma_start(
                                out=d_xbc[:, :, oc, :].transpose([1, 0, 2]),
                                in_=cvout[:].rearrange("p (c t) -> p c t", c=NC, t=CH)))

                @block.vector
                def _(v):
                    s_ev1.wait(v)
                    s_cv.inc(v.tensor_scalar_mul(ndt[:, 0:T], ndt[:, 0:T], -1.0))

            with ExitStack() as pa:
                ea = pa.enter_context
                sba = lambda n, s, d_: ea(nc.sbuf_tensor(n, list(s), d_))
                xb_a = sba("xb_a", (128, 2 * 32 * CH), BF16)
                xdtT_a = sba("xdtT_a", (128, D), BF16)
                bsdT_a = sba("bsdT_a", (128, D), BF16)
                cumT_a = sba("cumT_a", (32, 128), F32)
                sdT_a = sba("sdT_a", (32, 128), F32)
                sdtm_a = sba("sdtm_a", (128, 32), F32)
                cdc_a = sba("cdc_a", (32, 1), F32)
                rsel_a = sba("rsel_a", (32, 16), F32)
                cds_a = sba("cds_a", (128, 16), F32)
                gath_a = sba("gath_a", (128, 1040), F32)
                tt_a = sba("tt_a", (128, 1024), F32)
                ccst_a = sba("ccst_a", (128, 16), F32)
                cdcum_a = sba("cdcum_a", (128, 16), F32)
                ps_atr = [ea(nc.psum_tensor(f"ps_atr{i}", [128, 256], BF16))
                          for i in range(2)]
                ps_as = [ea(nc.psum_tensor(f"ps_as{i}", [128, 64], F32))
                         for i in range(2)]
                ps_asd = ea(nc.psum_tensor("ps_asd", [128, 48], F32))

                s_al = sem("s_al")
                s_apc = sem("s_apc")
                s_adv = sem("s_adv")
                s_aac = sem("s_aac")
                s_asp = sem("s_asp")
                s_cc = sem("s_cc")

                for c in range(NC):
                    @block.scalar
                    def _(s, c=c):
                        s_sp1.wait(s)
                        s_apc.wait(s)
                        s_al.dinc(s.dma_start(
                            out=xb_a[:, (c % 2) * 32 * CH:(c % 2 + 1) * 32 * CH],
                            in_=d_xbc[c][:, 0:32, :]
                            .rearrange("p b t -> p (b t)")))

                    @block.sync
                    def _(sync, c=c):
                        s_adv.wait(sync)
                        s_asp.dinc(sync.dma_start(
                            out=d_pvl[c],
                            in_=prev_ab[:, (c % 2) * 1024:(c % 2 + 1) * 1024]))

                    @block.vector
                    def _(v, c=c):
                        s_cv.wait(v)
                        s_ini.wait(v, INI_N)
                        s_adv.inc(v.tensor_tensor_scan(
                            cumT_a[:], ndt[:, c * CH:(c + 1) * CH], zeros32[:],
                            0.0, ALU.add, ALU.add))

                    @block.scalar
                    def _(s, c=c):
                        s_adv.wait(s)
                        s_aac.inc(s.activation(
                            out=sdT_a[:], in_=cumT_a[:], func=AF.Exp,
                            bias=cumT_a[:, CH - 1:CH], scale=cst_s[0:32, 2:3]))
                        s_aac.inc(s.activation(
                            out=cdc_a[:], in_=cumT_a[:, CH - 1:CH], func=AF.Exp))

                    @block.vector
                    def _(v, c=c):
                        s_aac.wait(v)
                        s_adv.inc(v.tensor_scalar(
                            rsel_a[:], selcd_s[:], cdc_a[:], None, ALU.mult))

                    @block.tensor
                    def _(t, c=c):
                        s_aac.wait(t)
                        s_adv.wait(t)
                        s_cl.wait(t)
                        s_apc.inc(t.transpose(ps_asd[:, 0:32], sdT_a[:], identf_s[0:32, 0:32]))
                        s_apc.inc(t.matmul(ps_asd[:, 32:48], halfsel_s[:], rsel_a[:],
                                           start=True, stop=True))

                    @block.vector
                    def _(v, c=c):
                        s_apc.wait(v)
                        s_adv.inc(v.tensor_copy(sdtm_a[:], ps_asd[:, 0:32]))
                        s_adv.inc(v.tensor_copy(cds_a[:], ps_asd[:, 32:48]))
                        s_adv.inc(v.tensor_copy(
                            cdpre_s[:, c * 16:(c + 1) * 16], cdp_s[:]))

                    for grp in range(8):
                        @block.tensor
                        def _(t, c=c, grp=grp):
                            s_al.wait(t)
                            s_adv.wait(t)
                            xb = xb_a[:, (c % 2) * 32 * CH:]
                            for k in range(4):
                                i = grp * 4 + k
                                s_apc.inc(t.transpose(
                                    ps_atr[k // 2][:, (k % 2) * 128:(k % 2) * 128 + 128],
                                    xb[:, i * CH:(i + 1) * CH], identbf_s[:]))

                        @block.vector
                        def _(v, c=c, grp=grp):
                            s_apc.wait(v)
                            for k in range(4):
                                i = grp * 4 + k
                                src = ps_atr[k // 2][:, (k % 2) * 128:(k % 2) * 128 + 128]
                                if grp < 4:
                                    s_adv.inc(v.tensor_copy(
                                        xdtT_a[:, (i % 16) * 128:(i % 16) * 128 + 128], src))
                                else:
                                    ii = i - 16
                                    sd3 = sdtm_a[:, 2 * ii:2 * ii + 2].unsqueeze(2) \
                                        .broadcast_to([128, 2, 64])
                                    s_adv.inc(v.tensor_tensor(
                                        bsdT_a[:, ii * 128:(ii + 1) * 128].rearrange(
                                            "p (a b) -> p a b", a=2, b=64),
                                        src.rearrange("p (a b) -> p a b", a=2, b=64),
                                        sd3, ALU.mult))

                    @block.sync
                    def _(sync, c=c):
                        s_adv.wait(sync)
                        s_asp.dinc(sync.dma_start(out=d_xdtT[c], in_=xdtT_a[:]))

                    for j in range(16):
                        @block.tensor
                        def _(t, c=c, j=j):
                            s_adv.wait(t)
                            for hh in range(2):
                                h = 2 * j + hh
                                s_apc.inc(t.matmul(
                                    ps_as[j % 2][0:64, :] if hh == 0 else ps_as[j % 2][64:128, :],
                                    bsdT_a[:, h * 64:(h + 1) * 64],
                                    xdtT_a[:, h * 64:(h + 1) * 64],
                                    start=True, stop=True,
                                    tile_position=(0, 0) if hh == 0 else (0, 64)))

                        @block.vector
                        def _(v, c=c, j=j):
                            s_apc.wait(v)
                            s_asp.wait(v)
                            src_b = prev_ab[:, (c % 2) * 1024:]
                            dst_b = prev_ab[:, ((c + 1) % 2) * 1024:]
                            s_adv.inc(v.scalar_tensor_tensor(
                                dst_b[:, j * 64:(j + 1) * 64],
                                src_b[:, j * 64:(j + 1) * 64],
                                cds_a[:, j:j + 1], ps_as[j % 2][:],
                                ALU.mult, ALU.add))

                    @block.vector
                    def _(v, c=c):
                        s_adv.wait(v)
                        s_adv.inc(v.tensor_tensor(cdp_s[:], cdp_s[:], cds_a[:], ALU.mult))

                @block.sync
                def _(sync):
                    s_adv.wait(sync)
                    s_cc.dinc(sync.dma_start(
                        out=d_ccin[:, 0:1024],
                        in_=prev_ab[:, (NC % 2) * 1024:(NC % 2) * 1024 + 1024]))
                    s_cc.dinc(sync.dma_start(out=d_ccin[:, 1024:1040], in_=cdp_s[:]))

                @block.gpsimd
                def _(g):
                    s_cc.wait(g)
                    s_clg.wait(g)
                    s_ccw.wait(g)
                    s_cc.inc(g.collective_compute(
                        "AllGather", ALU.bypass,
                        ins=[d_ccin[:]], outs=[d_ccout[:]],
                        replica_groups=[list(range(n_cores))]))

                @block.vector
                def _(v):
                    s_cl.wait(v)
                    s_ini.wait(v, INI_N)
                    s_adv.inc(v.memset(cdcum_a[:], 1.0))

                for j in range(n_cores - 2, -1, -1):
                    @block.scalar
                    def _(s, j=j):
                        s_cc.wait(s)
                        s_adv.wait(s)
                        s_cc.dinc(s.dma_start(
                            out=gath_a[:], in_=d_ccout[128 * j:128 * (j + 1)]))

                    @block.vector
                    def _(v, j=j):
                        s_cc.wait(v)
                        cd3 = cdcum_a[:].unsqueeze(2).broadcast_to([128, 16, 64])
                        s_adv.inc(v.tensor_tensor(
                            tt_a[:].rearrange("p (a b) -> p a b", a=16, b=64),
                            gath_a[:, 0:1024].rearrange("p (a b) -> p a b", a=16, b=64),
                            cd3, ALU.mult))
                        s_adv.wait(v)
                        s_adv.inc(v.scalar_tensor_tensor(
                            prefix_s[:], tt_a[:], pmask_s[:, j:j + 1], prefix_s[:],
                            ALU.mult, ALU.add))
                        s_adv.inc(v.tensor_scalar(
                            ccst_a[:], gath_a[:, 1024:1040], -1.0, pmask_s[:, j:j + 1],
                            ALU.add, ALU.mult))
                        s_adv.wait(v)
                        s_adv.inc(v.scalar_tensor_tensor(
                            cdcum_a[:], ccst_a[:], 1.0, cdcum_a[:], ALU.add, ALU.mult))

                if dbg:
                    @block.sync
                    def _(sync):
                        s_adv.wait(sync)
                        s_cv.wait(sync)
                        s_ev1.wait(sync)
                        s_cc.dinc(sync.dma_start(out=dbg_t["pref_dbg"], in_=prefix_s[:]))
                        s_cc.dinc(sync.dma_start(out=dbg_t["ndt_dbg"], in_=ndt[:, 0:T]))
                        s_cc.dinc(sync.dma_start(out=dbg_t["siluz_dbg"], in_=sz_s[:]))
                PREF_ADV = s_adv.n

            with ExitStack() as pb:
                eb = pb.enter_context
                sbb = lambda n, s, d_: eb(nc.sbuf_tensor(n, list(s), d_))
                bc_b = sbb("bc_b", (128, 2 * 32 * CH), BF16)
                xdt_b = sbb("xdt_b", (128, 2 * D), BF16)
                pvl_b = sbb("pvl_b", (128, 2 * 1024), F32)
                cumT_b = sbb("cumT_b", (32, 128), F32)
                cum_tm = sbb("cum_tm", (128, 32), F32)
                ngc_tm = sbb("ngc_tm", (128, 32), F32)
                ec_tm = sbb("ec_tm", (128, 32), F32)
                m1m_b = sbb("m1m_b", (128, 2 * 256), F32)
                dt_b = sbb("dt_b", (128, 2 * 256), BF16)
                mt_b = sbb("mt_b", (128, 2 * 256), BF16)
                prevc_b = sbb("prevc_b", (128, 1024), F32)
                prevbf_b = sbb("prevbf_b", (128, 1024), BF16)
                ts_b = sbb("ts_b", (128, 2 * 128), F32)
                ysb_b = sbb("ysb_b", (128, D), BF16)
                yz_b = sbb("yz_b", (128, D), BF16)
                ps_px = [eb(nc.psum_tensor(f"ps_px{i}", [128, 512], F32))
                         for i in range(2)]
                ps_py = [eb(nc.psum_tensor(f"ps_py{i}", [128, 320], F32))
                         for i in range(2)]
                ps_btr = [eb(nc.psum_tensor(f"ps_btr{i}", [128, 256], BF16))
                          for i in range(2)]
                ps_ms = eb(nc.psum_tensor("ps_ms", [128, 32], F32))

                s_bl = sem("s_bl")
                s_bpc = sem("s_bpc")
                s_bdv = sem("s_bdv")
                s_bac = sem("s_bac")
                s_bst = sem("s_bst")

                for c in range(NC):
                    @block.sync
                    def _(sync, c=c):
                        s_asp.wait(sync, 32 * NC)
                        if c == 0:
                            s_adv.wait(sync, PREF_ADV)
                            s_apc.wait(sync)
                            s_aac.wait(sync)
                            s_cc.wait(sync)
                        s_bpc.wait(sync)
                        s_bl.dinc(sync.dma_start(
                            out=bc_b[:, (c % 2) * 32 * CH:(c % 2 + 1) * 32 * CH],
                            in_=d_xbc[c][:, 16:48, :]
                            .rearrange("p b t -> p (b t)")))
                        s_bl.dinc(sync.dma_start(
                            out=xdt_b[:, (c % 2) * D:(c % 2 + 1) * D], in_=d_xdtT[c]))
                        s_bl.dinc(sync.dma_start(
                            out=pvl_b[:, (c % 2) * 1024:(c % 2 + 1) * 1024], in_=d_pvl[c]))

                    @block.vector
                    def _(v, c=c):
                        s_cv.wait(v)
                        s_adv.wait(v, PREF_ADV)
                        s_ini.wait(v, INI_N)
                        s_bdv.inc(v.tensor_tensor_scan(
                            cumT_b[:], ndt[:, c * CH:(c + 1) * CH], zeros32[:],
                            0.0, ALU.add, ALU.add))
                        s_bl.wait(v)
                        cd3 = cdpre_s[:, c * 16:(c + 1) * 16].unsqueeze(2) \
                            .broadcast_to([128, 16, 64])
                        s_bdv.inc(v.tensor_tensor(
                            prevc_b[:].rearrange("p (a b) -> p a b", a=16, b=64),
                            prefix_s[:].rearrange("p (a b) -> p a b", a=16, b=64),
                            cd3, ALU.mult))
                        s_bdv.wait(v)
                        s_bdv.inc(v.tensor_add(
                            prevc_b[:], prevc_b[:],
                            pvl_b[:, (c % 2) * 1024:(c % 2) * 1024 + 1024]))
                        s_bdv.wait(v)
                        s_bdv.inc(v.tensor_copy(prevbf_b[:], prevc_b[:]))

                    @block.tensor
                    def _(t, c=c):
                        s_bdv.wait(t)
                        s_bac.wait(t)
                        s_cl.wait(t)
                        s_bpc.inc(t.transpose(ps_ms[:, 0:32], cumT_b[:], identf_s[0:32, 0:32]))

                    @block.vector
                    def _(v, c=c):
                        s_bpc.wait(v)
                        s_bdv.inc(v.tensor_scalar_mul(ngc_tm[:], ps_ms[:, 0:32], -1.0))
                        s_bdv.inc(v.tensor_copy(cum_tm[:], ps_ms[:, 0:32]))

                    @block.scalar
                    def _(s, c=c):
                        s_bdv.wait(s)
                        s_bac.inc(s.activation(out=ec_tm[:], in_=cum_tm[:], func=AF.Exp))

                    xdtT = xdt_b[:, (c % 2) * D:]
                    bmk = {}

                    def pair_m1m(v, c, j):
                        cum2 = cum_tm[:, 2 * j:2 * j + 2].unsqueeze(2) \
                            .broadcast_to([128, 2, 128])
                        s_bdv.inc(v.tensor_tensor(
                            m1m_b[:, (j % 2) * 256:(j % 2) * 256 + 256].rearrange(
                                "p (a b) -> p a b", a=2, b=128),
                            maskt2_s[:].rearrange("p (a b) -> p a b", a=2, b=128),
                            cum2, ALU.add))
                        bmk[("m1m", c, j)] = s_bdv.n

                    @block.vector
                    def _(v, c=c):
                        s_bdv.wait(v)
                        pair_m1m(v, c, 0)
                        pair_m1m(v, c, 1)

                    for j in range(16):
                        b = j % 2

                        @block.tensor
                        def _(t, c=c, j=j, b=b):
                            t.wait_ge(s_bdv.h, bmk[("m1m", c, j)])
                            t.wait_ge(s_bdv.h, bmk.get(("comb", c, j - 2),
                                                       bmk.get(("combprev", c), 0)))
                            for hh, pp in ((0, ps_px[b]), (1, ps_py[b])):
                                s_bpc.inc(t.transpose(
                                    pp[:, 0:128],
                                    m1m_b[:, b * 256 + hh * 128:b * 256 + hh * 128 + 128],
                                    identf_s[:]))
                            bmk[("tr", c, j)] = s_bpc.n

                        @block.scalar
                        def _(s, c=c, j=j, b=b):
                            s.wait_ge(s_bpc.h, bmk[("tr", c, j)])
                            for hh, pp in ((0, ps_px[b]), (1, ps_py[b])):
                                h = 2 * j + hh
                                s_bac.inc(s.activation(
                                    out=dt_b[:, b * 256 + hh * 128:b * 256 + hh * 128 + 128],
                                    in_=pp[:, 0:128],
                                    func=AF.Exp, bias=ngc_tm[:, h:h + 1]))
                            bmk[("dt", c, j)] = s_bac.n

                        @block.tensor
                        def _(t, c=c, j=j, b=b):
                            bc = bc_b[:, (c % 2) * 32 * CH:]
                            if j >= 1:
                                jj = j - 1
                                bb = jj % 2
                                t.wait_ge(s_bdv.h, bmk[("mt", c, jj)])
                                for hh, pp in ((0, ps_px[bb]), (1, ps_py[bb])):
                                    h = 2 * jj + hh
                                    s_bpc.inc(t.matmul(
                                        ps_px[bb][:, 320 + hh * 64:320 + hh * 64 + 64],
                                        mt_b[:, bb * 256 + hh * 128:bb * 256 + hh * 128 + 128],
                                        xdtT[:, h * 64:(h + 1) * 64],
                                        start=True, stop=True))
                                    ct = bc[64 * hh:64 * hh + 64,
                                            (16 + jj) * CH:(17 + jj) * CH]
                                    pv = prevbf_b[64 * hh:64 * hh + 64,
                                                  jj * 64:(jj + 1) * 64]
                                    s_bpc.inc(t.matmul(
                                        pp[:, 256:320],
                                        ct, pv, start=True, stop=True,
                                        tile_position=(64 * hh, 0)))
                                bmk[("ymm", c, jj)] = s_bpc.n
                            t.wait_ge(s_bac.h, bmk[("dt", c, j)])
                            s_bl.wait(t)
                            for hh, pp in ((0, ps_px[b]), (1, ps_py[b])):
                                bt = bc[64 * hh:64 * hh + 64, j * CH:(j + 1) * CH]
                                ct = bc[64 * hh:64 * hh + 64,
                                        (16 + j) * CH:(17 + j) * CH]
                                s_bpc.inc(t.matmul(
                                    pp[:, 128:256],
                                    bt, ct, start=True, stop=True,
                                    tile_position=(64 * hh, 0)))
                            bmk[("cbt", c, j)] = s_bpc.n

                        @block.vector
                        def _(v, c=c, j=j, b=b):
                            if j >= 1:
                                jj = j - 1
                                bb = jj % 2
                                v.wait_ge(s_bpc.h, bmk[("ymm", c, jj)])
                                for hh, pp in ((0, ps_px[bb]), (1, ps_py[bb])):
                                    h = 2 * jj + hh
                                    s_bdv.inc(v.tensor_scalar(
                                        ts_b[:, bb * 128 + hh * 64:bb * 128 + hh * 64 + 64],
                                        pp[:, 256:320],
                                        ec_tm[:, h:h + 1], None, ALU.mult))
                                s_bdv.wait(v)
                                s_bdv.inc(v.tensor_add(
                                    ts_b[:, bb * 128:bb * 128 + 128],
                                    ts_b[:, bb * 128:bb * 128 + 128],
                                    ps_px[bb][:, 320:448]))
                                s_bdv.wait(v)
                                for hh in (0, 1):
                                    h = 2 * jj + hh
                                    s_bdv.inc(v.scalar_tensor_tensor(
                                        ysb_b[:, jj * 128 + hh * 64:jj * 128 + hh * 64 + 64],
                                        xdtT[:, jj * 128 + hh * 64:jj * 128 + hh * 64 + 64],
                                        dcol_s[:, h:h + 1],
                                        ts_b[:, bb * 128 + hh * 64:bb * 128 + hh * 64 + 64],
                                        ALU.mult, ALU.add))
                                bmk[("comb", c, jj)] = s_bdv.n
                            v.wait_ge(s_bpc.h, bmk[("cbt", c, j)])
                            for hh, pp in ((0, ps_px[b]), (1, ps_py[b])):
                                s_bdv.inc(v.tensor_tensor(
                                    mt_b[:, b * 256 + hh * 128:b * 256 + hh * 128 + 128],
                                    pp[:, 128:256],
                                    dt_b[:, b * 256 + hh * 128:b * 256 + hh * 128 + 128],
                                    ALU.mult))
                            bmk[("mt", c, j)] = s_bdv.n
                            if j + 2 < 16:
                                pair_m1m(v, c, j + 2)

                    @block.tensor
                    def _(t, c=c):
                        jj = 15
                        bb = jj % 2
                        bc = bc_b[:, (c % 2) * 32 * CH:]
                        t.wait_ge(s_bdv.h, bmk[("mt", c, jj)])
                        for hh, pp in ((0, ps_px[bb]), (1, ps_py[bb])):
                            h = 2 * jj + hh
                            s_bpc.inc(t.matmul(
                                ps_px[bb][:, 320 + hh * 64:320 + hh * 64 + 64],
                                mt_b[:, bb * 256 + hh * 128:bb * 256 + hh * 128 + 128],
                                xdtT[:, h * 64:(h + 1) * 64],
                                start=True, stop=True))
                            ct = bc[64 * hh:64 * hh + 64,
                                    (16 + jj) * CH:(17 + jj) * CH]
                            pv = prevbf_b[64 * hh:64 * hh + 64,
                                          jj * 64:(jj + 1) * 64]
                            s_bpc.inc(t.matmul(
                                pp[:, 256:320],
                                ct, pv, start=True, stop=True,
                                tile_position=(64 * hh, 0)))
                        bmk[("ymm", c, jj)] = s_bpc.n

                    @block.vector
                    def _(v, c=c):
                        jj = 15
                        bb = jj % 2
                        v.wait_ge(s_bpc.h, bmk[("ymm", c, jj)])
                        for hh, pp in ((0, ps_px[bb]), (1, ps_py[bb])):
                            h = 2 * jj + hh
                            s_bdv.inc(v.tensor_scalar(
                                ts_b[:, bb * 128 + hh * 64:bb * 128 + hh * 64 + 64],
                                pp[:, 256:320],
                                ec_tm[:, h:h + 1], None, ALU.mult))
                        s_bdv.wait(v)
                        s_bdv.inc(v.tensor_add(
                            ts_b[:, bb * 128:bb * 128 + 128],
                            ts_b[:, bb * 128:bb * 128 + 128],
                            ps_px[bb][:, 320:448]))
                        s_bdv.wait(v)
                        for hh in (0, 1):
                            h = 2 * jj + hh
                            s_bdv.inc(v.scalar_tensor_tensor(
                                ysb_b[:, jj * 128 + hh * 64:jj * 128 + hh * 64 + 64],
                                xdtT[:, jj * 128 + hh * 64:jj * 128 + hh * 64 + 64],
                                dcol_s[:, h:h + 1],
                                ts_b[:, bb * 128 + hh * 64:bb * 128 + hh * 64 + 64],
                                ALU.mult, ALU.add))
                        bmk[("comb", c, jj)] = s_bdv.n
                        bmk[("combprev", c + 1)] = s_bdv.n

                    for grp in range(8):
                        @block.tensor
                        def _(t, c=c, grp=grp):
                            s_bdv.wait(t)
                            for k in range(2):
                                i = grp * 2 + k
                                s_bpc.inc(t.transpose(
                                    ps_btr[k][:, 0:128],
                                    ysb_b[:, i * 128:(i + 1) * 128], identbf_s[:]))

                        @block.vector
                        def _(v, c=c, grp=grp):
                            s_bpc.wait(v)
                            s_ev1.wait(v)
                            s_bst.wait(v)
                            for k in range(2):
                                i = grp * 2 + k
                                s_bdv.inc(v.tensor_tensor(
                                    yz_b[:, i * 128:(i + 1) * 128],
                                    ps_btr[k][:, 0:128],
                                    sz_s[:, i * T + c * CH:i * T + c * CH + CH],
                                    ALU.mult))

                    @block.sync
                    def _(sync, c=c):
                        s_bdv.wait(sync)
                        s_bst.dinc(sync.dma_start(out=d_yz[c], in_=yz_b[:]))

                SSDB_DV = s_bdv.n
                SSDB_PC = s_bpc.n
                SSDB_AC = s_bac.n
                SSDB_ST = s_bst.n


        with ExitStack() as pc:
            ec_ = pc.enter_context
            sbc = lambda n, s, d_: ec_(nc.sbuf_tensor(n, list(s), d_))
            wo_c = sbc("wo_c", (128, HC * D), BF16)
            yz_c = sbc("yz_c", (128, 2 * D), BF16)
            u_c = sbc("u_c", (128, 2 * D), F32)
            h1bf_c = sbc("h1bf_c", (128, 2 * D), BF16)
            sexp_c = sbc("sexp_c", (128, E), F32)
            m8_c = sbc("m8_c", (128, 8), F32)
            st_c = sbc("st_c", (128, 32), F32)
            ps_wo = [ec_(nc.psum_tensor(f"ps_wo{i}", [128, 512], F32))
                     for i in range(2)]
            ps_ctr = [ec_(nc.psum_tensor(f"ps_ctr{i}", [128, 128], BF16))
                      for i in range(2)]
            ps_lg = ec_(nc.psum_tensor("ps_lg", [128, 8], F32))

            cmk = {}
            s_cw = sem("s_cw")
            s_cyz = sem("s_cyz")
            s_cu = sem("s_cu")
            s_cpc = sem("s_cpc")
            s_cdv = sem("s_cdv")
            s_cac = sem("s_cac")
            s_cst = sem("s_cst")

            @block.gpsimd
            def _(g):
                s_bdv.wait(g, SSDB_DV)
                s_bpc.wait(g, SSDB_PC)
                s_bac.wait(g, SSDB_AC)
                s_bst.wait(g, SSDB_ST)
                s_cw.dinc(g.dma_start(
                    out=wo_c[:].rearrange("p (ic o) -> p ic o", ic=HC, o=D),
                    in_=i_wot[:].transpose([1, 0, 2])))

            def emit_trgrps(cc):
                for grp in range(8):
                    @block.tensor
                    def _(t, cc=cc, grp=grp):
                        t.wait_ge(s_cdv.h, cmk[("h1c", cc)])
                        if grp > 0:
                            t.wait_ge(s_cdv.h, cmk[("h1Tg", cc, grp - 1)])
                        for k in range(2):
                            i = grp * 2 + k
                            s_cpc.inc(t.transpose(
                                ps_ctr[k][:],
                                h1bf_c[:, (cc % 2) * D + i * 128:
                                       (cc % 2) * D + (i + 1) * 128],
                                identbf_s[:]))

                    @block.vector
                    def _(v, cc=cc, grp=grp):
                        s_cpc.wait(v)
                        for k in range(2):
                            i = grp * 2 + k
                            s_cdv.inc(v.tensor_copy(
                                h1T[:, i * T + cc * CH:i * T + cc * CH + CH],
                                ps_ctr[k][:]))
                        cmk[("h1Tg", cc, grp)] = s_cdv.n
                        if grp == 7:
                            cmk[("h1T", cc)] = s_cdv.n

            def emit_loggate(cc):
                pg = 16 * (cc % 2)

                @block.tensor
                def _(t, cc=cc):
                    t.wait_ge(s_cdv.h, cmk[("h1T", cc)])
                    t.wait_ge(s_cac.h, cmk.get(("sexp", cc - 1), 0))
                    t.wait_ge(s_cdv.h, cmk.get(("gmax", cc - 1), 0))
                    s_clg.wait(t)
                    for ic in range(HC):
                        s_cpc.inc(t.matmul(
                            ps_lg[:],
                            h1T[:, ic * T + cc * CH:ic * T + cc * CH + CH],
                            gw_s[:, ic * E:(ic + 1) * E],
                            start=(ic == 0), stop=(ic == HC - 1)))

                @block.vector
                def _(v, cc=cc, pg=pg):
                    s_cpc.wait(v)
                    s_cdv.inc(v.reduce_max(st_c[:, pg + 13:pg + 14], ps_lg[:], AX.X))
                    cmk[("gmax", cc)] = s_cdv.n
                    s_cdv.wait(v)
                    s_cdv.inc(v.tensor_scalar_mul(
                        st_c[:, pg + 14:pg + 15], st_c[:, pg + 13:pg + 14], -1.0))

                @block.scalar
                def _(s, cc=cc, pg=pg):
                    s_cdv.wait(s)
                    s_cac.inc(s.activation(
                        out=sexp_c[:], in_=ps_lg[:], func=AF.Exp,
                        bias=st_c[:, pg + 14:pg + 15],
                        accum_out=st_c[:, pg + 15:pg + 16]))
                    cmk[("sexp", cc)] = s_cac.n

                @block.vector
                def _(v, cc=cc, pg=pg):
                    s_cac.wait(v)
                    s_cdv.inc(v.max(out=m8_c[:], in_=sexp_c[:]))
                    s_cdv.wait(v)
                    s_cdv.inc(v.tensor_add(
                        st_c[:, pg:pg + 1], m8_c[:, 0:1], m8_c[:, 1:2]))
                    s_cdv.wait(v)
                    s_cdv.inc(v.scalar_tensor_tensor(
                        st_c[:, pg + 1:pg + 2], st_c[:, pg + 15:pg + 16], 1.0e-20,
                        st_c[:, pg:pg + 1], ALU.mult, ALU.add))
                    s_cdv.wait(v)
                    s_cdv.inc(v.reciprocal(st_c[:, pg + 2:pg + 3],
                                           st_c[:, pg + 1:pg + 2]))
                    s_cdv.inc(v.tensor_scalar(
                        m8_c[:], sexp_c[:], m8_c[:, 1:2], None, ALU.is_ge))
                    s_cdv.wait(v)
                    s_cdv.inc(v.tensor_mul(sexp_c[:], sexp_c[:], m8_c[:]))
                    s_cdv.wait(v)
                    s_cdv.inc(v.tensor_scalar(
                        cw_s[:, cc * E:(cc + 1) * E], sexp_c[:],
                        st_c[:, pg + 2:pg + 3], None, ALU.mult))

            for c in range(NC):
                @block.sync
                def _(sync, c=c):
                    if c == 0:
                        s_bst.wait(sync, SSDB_ST)
                        s_bdv.wait(sync, SSDB_DV)
                        s_bac.wait(sync, SSDB_AC)
                        s_bpc.wait(sync, SSDB_PC)
                    sync.wait_ge(s_cpc.h, cmk.get(("mm", c - 2, 3), 0))
                    s_cyz.dinc(sync.dma_start(
                        out=yz_c[:, (c % 2) * D:(c % 2 + 1) * D], in_=d_yz[c]))
                    sync.wait_ge(s_cdv.h, cmk.get(("ev", c - 2, 3), 0))
                    s_cu.dinc(sync.dma_start(
                        out=u_c[:, (c % 2) * D:(c % 2 + 1) * D],
                        in_=i_utm[c * CH:(c + 1) * CH]))
                    cmk[("ul", c)] = s_cu.n

                po = 16 * (c % 2)
                for q in range(4):
                    @block.tensor
                    def _(t, c=c, q=q):
                        s_cw.wait(t)
                        t.wait_ge(s_cyz.h, 16 * (c + 1))
                        t.wait_ge(s_cdv.h, cmk.get(("ev",
                            c if q >= 2 else c - 1, (q - 2) % 4), 0))
                        yz = yz_c[:, (c % 2) * D:]
                        for ic in range(HC):
                            s_cpc.inc(t.matmul(
                                ps_wo[q % 2][:],
                                yz[:, ic * 128:(ic + 1) * 128],
                                wo_c[:, ic * D + q * 512:ic * D + q * 512 + 512],
                                start=(ic == 0), stop=(ic == HC - 1)))
                        cmk[("mm", c, q)] = s_cpc.n

                    @block.vector
                    def _(v, c=c, q=q, po=po):
                        v.wait_ge(s_cpc.h, cmk[("mm", c, q)])
                        v.wait_ge(s_cu.h, cmk[("ul", c)])
                        s_cdv.inc(v.scalar_tensor_tensor(
                            y_acc[:, c * D + q * 512:c * D + q * 512 + 512],
                            ps_wo[q % 2][:], 1.0,
                            u_c[:, (c % 2) * D + q * 512:(c % 2) * D + q * 512 + 512],
                            ALU.mult, ALU.add,
                            accum_out=st_c[:, po + q:po + q + 1]))
                        cmk[("ev", c, q)] = s_cdv.n

                    if q == 1 and c >= 1:
                        emit_trgrps(c - 1)
                    if q == 2 and c >= 1:
                        emit_loggate(c - 1)

                @block.vector
                def _(v, c=c, po=po):
                    s_cdv.wait(v)
                    s_cdv.inc(v.reduce_sum(st_c[:, po + 4:po + 5],
                                           st_c[:, po:po + 4], AX.X))
                    s_cdv.wait(v)
                    s_cdv.inc(v.tensor_scalar_mul(
                        st_c[:, po + 5:po + 6], st_c[:, po + 4:po + 5], 1.0 / D))

                @block.scalar
                def _(s, c=c, po=po):
                    s_cdv.wait(s)
                    s_cac.inc(s.activation(
                        out=h1bf_c[:, (c % 2) * D:(c % 2 + 1) * D],
                        in_=y_acc[:, c * D:(c + 1) * D], func=AF.Square,
                        accum_out=st_c[:, po + 6:po + 7]))

                @block.vector
                def _(v, c=c, po=po):
                    s_cac.wait(v)
                    s_cdv.inc(v.tensor_scalar_mul(
                        st_c[:, po + 7:po + 8], st_c[:, po + 6:po + 7], 1.0 / D))
                    s_cdv.inc(v.tensor_mul(
                        st_c[:, po + 8:po + 9], st_c[:, po + 5:po + 6],
                        st_c[:, po + 5:po + 6]))
                    s_cdv.wait(v)
                    s_cdv.inc(v.tensor_sub(
                        st_c[:, po + 9:po + 10], st_c[:, po + 7:po + 8],
                        st_c[:, po + 8:po + 9]))

                @block.scalar
                def _(s, c=c, po=po):
                    s_cdv.wait(s)
                    s_cac.inc(s.activation(out=st_c[:, po + 10:po + 11],
                                           in_=st_c[:, po + 9:po + 10],
                                           func=AF.Ln, bias=cst_s[:, 0:1]))
                    s_cac.wait(s)
                    s_cac.inc(s.activation(out=st_c[:, po + 11:po + 12],
                                           in_=st_c[:, po + 10:po + 11],
                                           func=AF.Exp, scale=cst_s[:, 1:2]))

                @block.vector
                def _(v, c=c, po=po):
                    s_cac.wait(v)
                    s_cdv.inc(v.scalar_tensor_tensor(
                        st_c[:, po + 12:po + 13], st_c[:, po + 5:po + 6], -1.0,
                        st_c[:, po + 11:po + 12], ALU.mult, ALU.mult))

                @block.scalar
                def _(s, c=c, po=po):
                    s_cdv.wait(s)
                    s_cac.inc(s.activation(
                        out=y_acc[:, c * D:(c + 1) * D],
                        in_=y_acc[:, c * D:(c + 1) * D], func=AF.Identity,
                        scale=st_c[:, po + 11:po + 12],
                        bias=st_c[:, po + 12:po + 13]))
                    cmk[("rln", c)] = s_cac.n

                @block.vector
                def _(v, c=c):
                    s_cac.wait(v)
                    s_cdv.inc(v.tensor_copy(
                        h1bf_c[:, (c % 2) * D:(c % 2 + 1) * D],
                        y_acc[:, c * D:(c + 1) * D]))
                    cmk[("h1c", c)] = s_cdv.n

                if dbg:
                    @block.sync
                    def _(sync, c=c):
                        s_cdv.wait(sync)
                        s_cst.dinc(sync.dma_start(
                            out=dbg_t["r_dbg"][c], in_=y_acc[:, c * D:(c + 1) * D]))

            emit_trgrps(NC - 1)
            emit_loggate(NC - 1)

            if dbg:
                @block.sync
                def _(sync):
                    s_cdv.wait(sync)
                    s_cac.wait(sync)
                    s_cst.dinc(sync.dma_start(out=dbg_t["h1_dbg"], in_=y_acc[:]))
                    s_cst.dinc(sync.dma_start(out=dbg_t["cw_dbg"], in_=cw_s[:]))

            SSDC_DV = s_cdv.n
            SSDC_PC = s_cpc.n
            SSDC_AC = s_cac.n
            SSDC_ST = s_cst.n

        with ExitStack() as pf:
            ef = pf.enter_context
            sbf = lambda n, s, d_: ef(nc.sbuf_tensor(n, list(s), d_))
            wg_f = sbf("wg_f", (128, 2 * D), BF16)
            wu_f = sbf("wu_f", (128, 2 * D), BF16)
            wd_f = sbf("wd_f", (128, FC * D), BF16)
            gu_f = sbf("gu_f", (128, 2 * T), BF16)
            gt_f = sbf("gt_f", (128, 2 * FC * CH), BF16)
            sg_f = sbf("sg_f", (128, 1024), F32)
            out_f = sbf("out_f", (128, D), F32)
            st_f = sbf("st_f", (128, 16), F32)
            ps_g = ef(nc.psum_tensor("ps_g", [128, 1024], F32))
            ps_u = ef(nc.psum_tensor("ps_u", [128, 1024], F32))
            ps_d = ef(nc.psum_tensor("ps_d", [128, 1024], F32))
            ps_d2 = ef(nc.psum_tensor("ps_d2", [128, 1024], F32))

            s_fw2 = [sem("s_fwA"), sem("s_fwB")]
            s_fd = sem("s_fd")
            s_fpc = sem("s_fpc")
            s_fdv = sem("s_fdv")
            s_fac = sem("s_fac")
            s_fsp2 = [sem("s_fspA"), sem("s_fspB")]
            s_fgl2 = [sem("s_fglA"), sem("s_fglB")]
            s_out = sem("s_out")

            EXS = [("shA", None), ("shB", None)] + [("e", e) for e in range(E)]

            def wgwu_src(xi, fc):
                nm, e = EXS[xi]
                if nm == "shA":
                    return i_swg[fc], i_swu[fc]
                if nm == "shB":
                    return i_swg[FC + fc], i_swu[FC + fc]
                return i_wg[e, fc], i_wu[e, fc]

            @block.gpsimd
            def _(g):
                s_cdv.wait(g, SSDC_DV)
                s_cpc.wait(g, SSDC_PC)
                s_cac.wait(g, SSDC_AC)
                s_cst.wait(g, SSDC_ST)
                for k in (0, 1):
                    sg_, su_ = wgwu_src(0, k)
                    s_fw2[k % 2].dinc(g.dma_start(
                        out=wg_f[:, (k % 2) * D:(k % 2 + 1) * D]
                        .rearrange("p (hc f) -> p hc f", hc=HC, f=128), in_=sg_))
                    s_fw2[k % 2].dinc(g.dma_start(
                        out=wu_f[:, (k % 2) * D:(k % 2 + 1) * D]
                        .rearrange("p (hc f) -> p hc f", hc=HC, f=128), in_=su_))

            NEX = len(EXS)
            TNL = [(i * 512, min(512, T - i * 512)) for i in range((T + 511) // 512)]
            mk = {}
            devac_mk = []

            def emit_unit(xi, fc):
                lin = xi * FC + fc

                @block.tensor
                def _(t, xi=xi, fc=fc, lin=lin):
                    t.wait_ge(s_fw2[lin % 2].h, 32 * (lin // 2 + 1))
                    if lin == 0:
                        s_cdv.wait(t, SSDC_DV)
                        s_cpc.wait(t, SSDC_PC)
                    for tn, (ts_, w_) in enumerate(TNL):
                        t.wait_ge(s_fac.h, mk.get(("silu", lin - 1, tn), 0))
                        t.wait_ge(s_fdv.h, mk.get(("gmul", lin - 1, tn), 0))
                        for hc in range(HC):
                            s_fpc.inc(t.matmul(
                                ps_g[:, ts_:ts_ + w_],
                                wg_f[:, (lin % 2) * D + hc * 128:
                                     (lin % 2) * D + hc * 128 + 128],
                                h1T[:, hc * T + ts_:hc * T + ts_ + w_],
                                start=(hc == 0), stop=(hc == HC - 1)))
                        mk[("mmg", lin, tn)] = s_fpc.n
                        for hc in range(HC):
                            s_fpc.inc(t.matmul(
                                ps_u[:, ts_:ts_ + w_],
                                wu_f[:, (lin % 2) * D + hc * 128:
                                     (lin % 2) * D + hc * 128 + 128],
                                h1T[:, hc * T + ts_:hc * T + ts_ + w_],
                                start=(hc == 0), stop=(hc == HC - 1)))
                        mk[("mmu", lin, tn)] = s_fpc.n

                if lin + 2 < NEX * FC:
                    @block.gpsimd
                    def _(g, lin=lin):
                        s_fpc.wait(g)
                        nxi, nfc = divmod(lin + 2, FC)
                        sg_, su_ = wgwu_src(nxi, nfc)
                        s_fw2[lin % 2].dinc(g.dma_start(
                            out=wg_f[:, (lin % 2) * D:(lin % 2 + 1) * D]
                            .rearrange("p (hc f) -> p hc f", hc=HC, f=128), in_=sg_))
                        s_fw2[lin % 2].dinc(g.dma_start(
                            out=wu_f[:, (lin % 2) * D:(lin % 2 + 1) * D]
                            .rearrange("p (hc f) -> p hc f", hc=HC, f=128), in_=su_))

                for tn, (ts_, w_) in enumerate(TNL):
                    @block.scalar
                    def _(s, lin=lin, tn=tn, ts_=ts_, w_=w_):
                        s.wait_ge(s_fpc.h, mk[("mmg", lin, tn)])
                        s.wait_ge(s_fdv.h, mk.get(("gmul", lin - 1, tn), 0))
                        s_fac.inc(s.activation(
                            out=sg_f[:, ts_:ts_ + w_],
                            in_=ps_g[:, ts_:ts_ + w_], func=AF.Silu))
                        mk[("silu", lin, tn)] = s_fac.n

                    @block.vector
                    def _(v, lin=lin, tn=tn, ts_=ts_, w_=w_):
                        v.wait_ge(s_fac.h, mk[("silu", lin, tn)])
                        v.wait_ge(s_fpc.h, mk[("mmu", lin, tn)])
                        sp = mk.get(("spill", lin - 2), (0, 0))
                        v.wait_ge(s_fsp2[lin % 2].h, sp[lin % 2])
                        s_fdv.inc(v.tensor_tensor(
                            gu_f[:, (lin % 2) * T + ts_:(lin % 2) * T + ts_ + w_],
                            sg_f[:, ts_:ts_ + w_],
                            ps_u[:, ts_:ts_ + w_], ALU.mult))
                        mk[("gmul", lin, tn)] = s_fdv.n

                @block.sync
                def _(sync, xi=xi, fc=fc, lin=lin):
                    sync.wait_ge(s_fdv.h, mk[("gmul", lin, len(TNL) - 1)])
                    gtls = mk.get(("gtls", xi - 2), (0, 0))
                    sync.wait_ge(s_fgl2[0].h, gtls[0])
                    sync.wait_ge(s_fgl2[1].h, gtls[1])
                    s_fsp2[lin % 2].dinc(sync.dma_start(
                        out=d_g[xi % 2][:, :, fc, :].transpose([1, 0, 2]),
                        in_=gu_f[:, (lin % 2) * T:(lin % 2 + 1) * T]
                        .rearrange("p (c t) -> p c t", c=NC, t=CH)))
                    mk[("spill", lin)] = (s_fsp2[0].n, s_fsp2[1].n)
                if fc == FC - 1:
                    mk[("spills", xi)] = (s_fsp2[0].n, s_fsp2[1].n)

            def emit_down_tc(y, tc):
                nm, ex = EXS[y]
                dg = y * NC + tc

                @block.scalar
                def _(s, y=y, tc=tc, dg=dg):
                    s.wait_ge(s_fsp2[0].h, mk[("spills", y)][0])
                    s.wait_ge(s_fsp2[1].h, mk[("spills", y)][1])
                    s.wait_ge(s_fpc.h, mk.get(("down_mm", dg - 2), 0))
                    s_fgl2[dg % 2].dinc(s.dma_start(
                        out=gt_f[:, (tc % 2) * FC * CH:(tc % 2 + 1) * FC * CH],
                        in_=d_g[y % 2, tc]
                        .rearrange("p fc t -> p (fc t)")))
                    mk[("gtl", dg)] = s_fgl2[dg % 2].n
                    if tc == NC - 1:
                        mk[("gtls", y)] = (s_fgl2[0].n, s_fgl2[1].n)

                for half in range(2):
                    @block.tensor
                    def _(t, y=y, tc=tc, dg=dg, half=half):
                        t.wait_ge(s_fgl2[dg % 2].h, mk[("gtl", dg)])
                        t.wait_ge(s_fd.h, 16 * FC * (y + 1))
                        if len(devac_mk) >= 2:
                            t.wait_ge(s_fdv.h, devac_mk[-2])
                        gt = gt_f[:, (tc % 2) * FC * CH:]
                        pd = ps_d if half == 0 else ps_d2
                        for hn in (2 * half, 2 * half + 1):
                            for fc in range(FC):
                                s_fpc.inc(t.matmul(
                                    pd[:, (hn % 2) * 512:(hn % 2) * 512 + 512],
                                    gt[:, fc * CH:(fc + 1) * CH],
                                    wd_f[:, fc * D + hn * 512:fc * D + hn * 512 + 512],
                                    start=(fc == 0), stop=(fc == FC - 1)))
                            mk[("dmm", dg, hn)] = s_fpc.n
                        if half == 1:
                            mk[("down_mm", dg)] = s_fpc.n

                    @block.vector
                    def _(v, y=y, nm=nm, ex=ex, tc=tc, dg=dg, half=half):
                        pd = ps_d if half == 0 else ps_d2
                        for hn in (2 * half, 2 * half + 1):
                            v.wait_ge(s_fpc.h, mk[("dmm", dg, hn)])
                            dst = y_acc[:, tc * D + hn * 512:tc * D + hn * 512 + 512]
                            if nm in ("shA", "shB"):
                                s_fdv.inc(v.tensor_add(
                                    dst, dst,
                                    pd[:, (hn % 2) * 512:(hn % 2) * 512 + 512]))
                            else:
                                s_fdv.inc(v.scalar_tensor_tensor(
                                    dst,
                                    pd[:, (hn % 2) * 512:(hn % 2) * 512 + 512],
                                    cw_s[:, tc * E + ex:tc * E + ex + 1],
                                    dst, ALU.mult, ALU.add))
                        devac_mk.append(s_fdv.n)

            def emit_wd(y, grp_):
                @block.gpsimd
                def _(g, y=y, grp_=grp_):
                    g.wait_ge(s_fpc.h, mk.get(("down_mm", y * NC - 1), 0))
                    nm2, ex2 = EXS[y]
                    if nm2 == "shA":
                        src = i_swd
                    elif nm2 == "shB":
                        src = i_swd[FC:]
                    else:
                        src = i_wd[ex2]
                    for k in grp_:
                        s_fd.dinc(g.dma_start(
                            out=wd_f[:, k * D:(k + 1) * D], in_=src[k]))

            for xi in range(NEX):
                for fc in range(FC):
                    emit_unit(xi, fc)
                    if xi >= 1 and 2 <= fc < NC + 2:
                        emit_down_tc(xi - 1, fc - 2)
                    if fc == FC - 1:
                        emit_wd(xi, range(0, 4))
                    if xi >= 1 and fc == 0:
                        emit_wd(xi - 1, range(4, 8))
                    if xi >= 1 and fc == 1:
                        emit_wd(xi - 1, range(8, FC))
            emit_wd(NEX - 1, range(4, 8))
            emit_wd(NEX - 1, range(8, FC))
            for tc in range(NC):
                emit_down_tc(NEX - 1, tc)

            for tc in range(NC):
                @block.vector
                def _(v, tc=tc):
                    s_fdv.wait(v)
                    s_fdv.inc(v.reduce_sum(st_f[:, 0:1],
                                           y_acc[:, tc * D:(tc + 1) * D], AX.X))
                    s_fdv.wait(v)
                    s_fdv.inc(v.tensor_scalar_mul(st_f[:, 1:2], st_f[:, 0:1], 1.0 / D))

                @block.scalar
                def _(s, tc=tc):
                    s_fdv.wait(s)
                    s_fac.wait(s)
                    s_out.wait(s)
                    s_fac.inc(s.activation(out=out_f[:], in_=y_acc[:, tc * D:(tc + 1) * D],
                                           func=AF.Square, accum_out=st_f[:, 2:3]))

                @block.vector
                def _(v, tc=tc):
                    s_fac.wait(v)
                    s_fdv.inc(v.tensor_scalar_mul(st_f[:, 3:4], st_f[:, 2:3], 1.0 / D))
                    s_fdv.inc(v.tensor_mul(st_f[:, 4:5], st_f[:, 1:2], st_f[:, 1:2]))
                    s_fdv.wait(v)
                    s_fdv.inc(v.tensor_sub(st_f[:, 5:6], st_f[:, 3:4], st_f[:, 4:5]))

                @block.scalar
                def _(s, tc=tc):
                    s_fdv.wait(s)
                    s_fac.inc(s.activation(out=st_f[:, 6:7], in_=st_f[:, 5:6],
                                           func=AF.Ln, bias=cst_s[:, 0:1]))
                    s_fac.wait(s)
                    s_fac.inc(s.activation(out=st_f[:, 7:8], in_=st_f[:, 6:7],
                                           func=AF.Exp, scale=cst_s[:, 1:2]))

                @block.vector
                def _(v, tc=tc):
                    s_fac.wait(v)
                    s_fdv.inc(v.scalar_tensor_tensor(
                        st_f[:, 8:9], st_f[:, 1:2], -1.0, st_f[:, 7:8],
                        ALU.mult, ALU.mult))

                @block.scalar
                def _(s, tc=tc):
                    s_fdv.wait(s)
                    s_fac.inc(s.activation(
                        out=out_f[:], in_=y_acc[:, tc * D:(tc + 1) * D], func=AF.Identity,
                        scale=st_f[:, 7:8], bias=st_f[:, 8:9]))

                @block.sync
                def _(sync, tc=tc):
                    s_fac.wait(sync)
                    s_out.dinc(sync.dma_start(
                        out=o_out[tc * CH:(tc + 1) * CH, :], in_=out_f[:]))

            @block.sync
            def _(sync):
                s_out.wait(sync)
                if dbg:
                    s_bst.wait(sync)
                    s_cc.wait(sync)

    return nc


def _prep(inputs, n_cores, T):
    f32 = np.float32
    u = np.asarray(inputs["L0_attn_u"], f32)[0]
    W_in = np.asarray(inputs["L0_attn_W_in"], f32)
    conv_w = np.asarray(inputs["L0_attn_conv_w"], f32)
    conv_b = np.asarray(inputs["L0_attn_conv_b"], f32)
    Dv = np.asarray(inputs["L0_attn_D"], f32)
    z_bias = np.asarray(inputs["L0_attn_z_bias"], f32)
    W_out = np.asarray(inputs["L0_attn_W_out"], f32)
    gate_w = np.asarray(inputs["L0_ffn_gate_weight"], f32)
    w_gate = np.asarray(inputs["L0_ffn_w_gate"], f32)
    w_up = np.asarray(inputs["L0_ffn_w_up"], f32)
    w_down = np.asarray(inputs["L0_ffn_w_down"], f32)
    sw_gate = np.asarray(inputs["L0_ffn_sw_gate"], f32)
    sw_up = np.asarray(inputs["L0_ffn_sw_up"], f32)
    sw_down = np.asarray(inputs["L0_ffn_sw_down"], f32)

    Wp = np.zeros((NOC * 128, D), f32)
    Wp[:8224] = W_in
    w_in_t = np.ascontiguousarray(
        Wp.reshape(NOC, 128, HC, 128).transpose(0, 3, 2, 1))

    conv_w_c = np.ascontiguousarray(
        conv_w.reshape(48, 128, 4).transpose(1, 0, 2).reshape(128, 48 * 4))
    conv_b_c = np.ascontiguousarray(conv_b.reshape(48, 128).T)
    d_col = np.ascontiguousarray(np.tile(Dv[None, :], (128, 1)))
    zb_c = np.ascontiguousarray(z_bias.reshape(16, 128).T)
    w_out_t = np.ascontiguousarray(W_out.T).reshape(HC, 128, D)
    gw_t = np.ascontiguousarray(gate_w.T).reshape(HC, 128, E)
    w_gate_t = np.ascontiguousarray(
        w_gate.reshape(E, FC, 128, HC, 128).transpose(0, 1, 4, 3, 2))
    w_up_t = np.ascontiguousarray(
        w_up.reshape(E, FC, 128, HC, 128).transpose(0, 1, 4, 3, 2))
    w_down_t = np.ascontiguousarray(w_down.transpose(0, 2, 1)).reshape(E, FC, 128, D)
    sw_gate_t = np.ascontiguousarray(
        sw_gate.reshape(FSC, 128, HC, 128).transpose(0, 3, 2, 1))
    sw_up_t = np.ascontiguousarray(
        sw_up.reshape(FSC, 128, HC, 128).transpose(0, 3, 2, 1))
    sw_down_t = np.ascontiguousarray(sw_down.T).reshape(FSC, 128, D)

    shared = dict(w_in_t=w_in_t, conv_w_c=conv_w_c, conv_b_c=conv_b_c,
                  d_col=d_col, zb_c=zb_c, w_out_t=w_out_t, gw_t=gw_t,
                  w_gate_t=w_gate_t, w_up_t=w_up_t, w_down_t=w_down_t,
                  sw_gate_t=sw_gate_t, sw_up_t=sw_up_t, sw_down_t=sw_down_t)

    in_maps = []
    for c in range(n_cores):
        s = c * T
        seg = np.zeros((T + 3, D), f32)
        lo = max(0, s - 3)
        seg[3 - (s - lo):] = u[lo:s + T]
        u_fm = np.ascontiguousarray(seg.T).reshape(HC, 128, T + 3)
        u_tm = np.ascontiguousarray(u[s:s + T])
        pm = np.zeros((128, NCORES), f32)
        pm[:, :c] = 1.0
        m = dict(shared)
        m.update(u_fm=u_fm, u_tm=u_tm, pmask=pm)
        in_maps.append(m)
    return in_maps


_CACHE = {}


def kernel(**inputs):
    T = 8192 // NCORES
    key = (T, NCORES)
    if key not in _CACHE:
        _CACHE[key] = build(T, NCORES)
    nc = _CACHE[key]
    in_maps = _prep(inputs, NCORES, T)
    res = run_bass_kernel_spmd(nc, in_maps, core_ids=list(range(NCORES)))
    out = np.concatenate([r["out"] for r in res.results], axis=0)
    return out.reshape(1, 8192, D).astype(np.float32)


# version 22 (final)
# speedup vs baseline: 1.0137x; 1.0137x over previous
import numpy as np
import ml_dtypes
from contextlib import ExitStack

import concourse.bass as bass
import concourse.mybir as mybir
from concourse.bass_utils import run_bass_kernel_spmd

F32 = mybir.dt.float32
BF16 = mybir.dt.bfloat16
AF = mybir.ActivationFunctionType
ALU = mybir.AluOpType
AX = mybir.AxisListType
BF = ml_dtypes.bfloat16

NCORES = 8
D = 2048
NH = 32
DS = 64
CONV_DIM = 6144
NOC = 65
E = 8
FC = 11
FSC = 22
HC = 16
CH = 128
NEG = -1.0e30
LNEPS = 1e-5


def _nch(total, m=512):
    out, s = [], 0
    while s < total:
        out.append((s, min(m, total - s)))
        s += m
    return out


class Sem:
    def __init__(self, ctx, nc, name):
        self.h = ctx.enter_context(nc.semaphore(name))
        self.n = 0

    def inc(self, inst, k=1):
        inst.then_inc(self.h, k)
        self.n += k
        return inst

    def dinc(self, inst):
        return self.inc(inst, 16)

    def wait(self, eng, val=None):
        eng.wait_ge(self.h, self.n if val is None else val)


def build(T, n_cores, dbg=False):
    NC = T // CH
    TH = T + 3
    nc = bass.Bass("TRN2", target_bir_lowering=False, num_devices=n_cores)

    din = lambda n, s: nc.dram_tensor(n, list(s), F32, kind="ExternalInput").ap()
    i_ufm = din("u_fm", (HC, 128, TH))
    i_utm = din("u_tm", (T, D))
    i_wint = din("w_in_t", (NOC, 128, HC, 128))
    i_convw = din("conv_w_c", (128, 48 * 4))
    i_convb = din("conv_b_c", (128, 48))
    i_dcol = din("d_col", (128, NH))
    i_zb = din("zb_c", (128, 16))
    i_wot = din("w_out_t", (HC, 128, D))
    i_gwt = din("gw_t", (HC, 128, E))
    i_wg = din("w_gate_t", (E, FC, 128, HC, 128))
    i_wu = din("w_up_t", (E, FC, 128, HC, 128))
    i_wd = din("w_down_t", (E, FC, 128, D))
    i_swg = din("sw_gate_t", (FSC, 128, HC, 128))
    i_swu = din("sw_up_t", (FSC, 128, HC, 128))
    i_swd = din("sw_down_t", (FSC, 128, D))
    i_pmask = din("pmask", (128, NCORES))
    o_out = nc.dram_tensor("out", [T, D], F32, kind="ExternalOutput").ap()

    sk = "ExternalOutput" if dbg else "Internal"
    d_xbc = nc.dram_tensor("xbc_sp", [NC, 128, 48, CH], BF16, kind=sk).ap()
    d_xdtT = nc.dram_tensor("xdtT_sp", [NC, 128, D], BF16, kind=sk).ap()
    d_pvl = nc.dram_tensor("prevloc_sp", [NC, 128, 1024], F32, kind=sk).ap()
    d_g = nc.dram_tensor("g_sp", [2, NC, 128, FC, CH], BF16, kind="Internal").ap()
    d_yz = nc.dram_tensor("yz_sp", [NC, 128, D], BF16, kind=sk).ap()
    d_ccin = nc.dram_tensor("cc_in", [128, 1040], F32, kind="Internal").ap()
    d_ccout = nc.dram_tensor("cc_out", [128 * n_cores, 1040], F32, kind="Internal").ap()
    d_wrm_in = nc.dram_tensor("wrm_in", [1, 8], F32, kind="Internal").ap()
    d_wrm_out = nc.dram_tensor("wrm_out", [n_cores, 8], F32, kind="Internal").ap()
    dbg_t = {}
    if dbg:
        for nm, sh in [("h1_dbg", (128, NC * D)), ("cw_dbg", (128, NC * E)),
                       ("siluz_dbg", (128, 16 * T)), ("ndt_dbg", (32, T)),
                       ("pref_dbg", (128, 1024)),
                       ("r_dbg", (NC, 128, D))]:
            dbg_t[nm] = nc.dram_tensor(nm, list(sh),
                                       BF16 if nm in ("siluz_dbg",) else F32,
                                       kind="ExternalOutput").ap()

    c_identbf = nc.inline_tensor(np.eye(128, dtype=np.float32).astype(BF), "c_identbf").ap()
    c_identf = nc.inline_tensor(np.eye(128, dtype=np.float32), "c_identf").ap()
    maskt_np = np.where(np.arange(128)[None, :] <= np.arange(128)[:, None], 0.0, NEG)
    c_maskt = nc.inline_tensor(maskt_np.astype(np.float32), "c_maskt").ap()
    c_maskt2 = nc.inline_tensor(
        np.hstack([maskt_np, maskt_np]).astype(np.float32), "c_maskt2").ap()
    c_selcd = nc.inline_tensor(
        (np.arange(NH)[:, None] // 2 == np.arange(16)[None, :]).astype(np.float32),
        "c_selcd").ap()
    c_halfsel = nc.inline_tensor(
        (np.arange(128)[None, :] // 64 == np.arange(NH)[:, None] % 2).astype(np.float32),
        "c_halfsel").ap()
    c_cst = nc.inline_tensor(
        np.tile(np.array([[LNEPS, -0.5, -1.0, 0.0]], np.float32), (128, 1)),
        "c_cst").ap()

    ctx = ExitStack()
    with ctx:
        en = ctx.enter_context
        sb = lambda n, s, d_: en(nc.sbuf_tensor(n, list(s), d_))

        y_acc = en(nc.sbuf_tensor("y_acc", [128, NC * D], F32, side="right"))
        h1T = en(nc.sbuf_tensor("h1T", [128, HC * T], BF16, side="right"))
        cw_s = sb("cw_s", (128, NC * E), F32)
        ndt = sb("ndt", (32, T), F32)
        convw_s = sb("convw_s", (128, 48 * 4), F32)
        convb_s = sb("convb_s", (128, 48), F32)
        dcol_s = sb("dcol_s", (128, NH), F32)
        zb_s = sb("zb_s", (128, 16), F32)
        gw_s = sb("gw_s", (128, HC * E), BF16)
        identbf_s = sb("identbf_s", (128, 128), BF16)
        identf_s = sb("identf_s", (128, 128), F32)
        maskt_s = sb("maskt_s", (128, 128), F32)
        maskt2_s = sb("maskt2_s", (128, 256), F32)
        selcd_s = sb("selcd_s", (32, 16), F32)
        halfsel_s = sb("halfsel_s", (32, 128), F32)
        zeros32 = sb("zeros32", (32, 128), F32)
        pmask_s = sb("pmask_s", (128, NCORES), F32)
        cst_s = sb("cst_s", (128, 4), F32)

        sem = lambda n: Sem(ctx, nc, n)
        s_cl = sem("s_cl")
        s_clg = sem("s_clg")
        s_ini = sem("s_ini")
        s_ccw = sem("s_ccw")
        block = en(nc.Block())

        @block.sync
        def _(sync):
            for dst, src in [(convw_s, i_convw), (convb_s, i_convb),
                             (dcol_s, i_dcol), (zb_s, i_zb),
                             (identf_s, c_identf), (maskt_s, c_maskt),
                             (maskt2_s, c_maskt2),
                             (selcd_s, c_selcd), (halfsel_s, c_halfsel),
                             (pmask_s, i_pmask), (cst_s, c_cst)]:
                s_cl.dinc(sync.dma_start(out=dst[:], in_=src[:]))

        @block.gpsimd
        def _(g):
            s_clg.dinc(g.dma_start(out=identbf_s[:], in_=c_identbf[:]))
            s_clg.dinc(g.dma_start(
                out=gw_s[:].rearrange("p (hc e) -> p hc e", hc=HC, e=E),
                in_=i_gwt[:].transpose([1, 0, 2])))
            s_ccw.inc(g.collective_compute(
                "AllGather", ALU.bypass,
                ins=[d_wrm_in[:]], outs=[d_wrm_out[:]],
                replica_groups=[list(range(n_cores))]))

        @block.vector
        def _(v):
            s_ini.inc(v.memset(zeros32[:], 0.0))

        ssd_scope = ExitStack()
        es = ssd_scope.enter_context
        sbs = lambda n, s, d_: es(nc.sbuf_tensor(n, list(s), d_))
        siluz = None

        with ssd_scope:
            sz_s = sbs("sz_s", (128, 16 * T), BF16)
            prefix_s = sbs("prefix_s", (128, 1024), F32)
            cdpre_s = sbs("cdpre_s", (128, 16 * NC), F32)
            cdp_s = sbs("cdp_s", (128, 16), F32)
            prev_ab = sbs("prev_ab", (128, 2 * 1024), F32)

            @block.vector
            def _(v):
                s_ini.inc(v.memset(prefix_s[:], 0.0))
                s_ini.inc(v.memset(prev_ab[:], 0.0))
                s_ini.inc(v.memset(cdp_s[:], 1.0))
            INI_N = s_ini.n

            @block.tensor
            def _(t):
                s_cl.wait(t); s_clg.wait(t); s_ini.wait(t, INI_N)

            @block.vector
            def _(v):
                s_cl.wait(v); s_clg.wait(v)

            @block.scalar
            def _(s):
                s_cl.wait(s); s_clg.wait(s); s_ini.wait(s, INI_N)

            @block.gpsimd
            def _(g):
                s_cl.wait(g); s_ini.wait(g, INI_N)

            @block.sync
            def _(sync):
                s_cl.wait(sync); s_clg.wait(sync); s_ini.wait(sync, INI_N)

            with ExitStack() as p1:
                e1 = p1.enter_context
                sb1 = lambda n, s, d_: e1(nc.sbuf_tensor(n, list(s), d_))
                u_bf = sb1("u_bf", (128, HC * TH), BF16)
                wbuf = sb1("wbuf", (128, 2 * HC * 128), BF16)
                evbuf = sb1("evbuf", (128, 2 * TH), F32)
                cvt0 = sb1("cvt0", (128, T), F32)
                cvt1 = sb1("cvt1", (128, T), F32)
                cvout = sb1("cvout", (128, T), BF16)
                ps_x = [e1(nc.psum_tensor(f"ps_x{i}", [128, 1056], F32))
                        for i in range(2)]

                s_wl2 = [sem("s_wlA"), sem("s_wlB")]
                s_ul = sem("s_ul")
                s_mm1 = sem("s_mm1")
                s_ev1 = sem("s_ev1")
                s_cv = sem("s_cv")
                s_sp1 = sem("s_sp1")

                @block.gpsimd
                def _(g):
                    for dc in range(HC):
                        s_ul.dinc(g.dma_start(
                            out=u_bf[:, dc * TH:(dc + 1) * TH], in_=i_ufm[dc]))
                    for oc in (0, 1):
                        s_wl2[oc % 2].dinc(g.dma_start(
                            out=wbuf[:, (oc % 2) * HC * 128:(oc % 2 + 1) * HC * 128]
                            .rearrange("p (dc o) -> p dc o", dc=HC, o=128),
                            in_=i_wint[oc]))

                for oc in range(NOC):
                    n_tok = TH if oc < 48 else T
                    wl_need = 16 * (oc // 2 + 1)
                    ev_need = max(0, oc - 1)

                    @block.tensor
                    def _(t, oc=oc, n_tok=n_tok, wl_need=wl_need, ev_need=ev_need):
                        t.wait_ge(s_wl2[oc % 2].h, wl_need)
                        t.wait_ge(s_ev1.h, ev_need)
                        if oc == 0:
                            t.wait_ge(s_ul.h, 16 * HC)
                        wb = wbuf[:, (oc % 2) * HC * 128:]
                        ps = ps_x[oc % 2]
                        off = 0 if oc < 48 else 3
                        ins = None
                        for (ns, nw) in _nch(n_tok):
                            for dc in range(HC):
                                ins = t.matmul(
                                    ps[:, ns:ns + nw],
                                    wb[:, dc * 128:(dc + 1) * 128],
                                    u_bf[:, dc * TH + off + ns:dc * TH + off + ns + nw],
                                    start=(dc == 0), stop=(dc == HC - 1))
                        s_mm1.inc(ins)

                    if oc + 2 < NOC:
                        @block.gpsimd
                        def _(g, oc=oc):
                            s_mm1.wait(g)
                            s_wl2[oc % 2].dinc(g.dma_start(
                                out=wbuf[:, (oc % 2) * HC * 128:(oc % 2 + 1) * HC * 128]
                                .rearrange("p (dc o) -> p dc o", dc=HC, o=128),
                                in_=i_wint[oc + 2]))

                    @block.scalar
                    def _(s, oc=oc, n_tok=n_tok):
                        s_mm1.wait(s)
                        ps = ps_x[oc % 2]
                        if oc < 48:
                            s_cv.wait(s)
                            s_cl.wait(s)
                            s_ev1.inc(s.copy(
                                out=evbuf[:, (oc % 2) * TH:(oc % 2) * TH + n_tok],
                                in_=ps[:, 0:n_tok]))
                        elif oc < 64:
                            zi = oc - 48
                            s_ev1.inc(s.activation(
                                out=sz_s[:, zi * T:(zi + 1) * T],
                                in_=ps[:, 0:n_tok], func=AF.Silu,
                                bias=zb_s[:, zi:zi + 1]))
                        else:
                            s_ev1.inc(s.activation(
                                out=ndt[:, 0:T], in_=ps[0:32, 0:n_tok], func=AF.Exp))
                            s.wait_ge(s_ev1.h, s_ev1.n)
                            s_ev1.inc(s.activation(
                                out=ndt[:, 0:T], in_=ndt[:, 0:T],
                                func=AF.Ln, bias=1.0))

                    if oc < 48:
                        @block.vector
                        def _(v, oc=oc):
                            s_ev1.wait(v)
                            s_sp1.wait(v)
                            ev = evbuf[:, (oc % 2) * TH:]
                            w = lambda k: convw_s[:, oc * 4 + k:oc * 4 + k + 1]
                            s_cv.inc(v.tensor_scalar(
                                cvt0[:], ev[:, 0:T], w(0), convb_s[:, oc:oc + 1],
                                ALU.mult, ALU.add))
                            s_cv.wait(v)
                            s_cv.inc(v.scalar_tensor_tensor(
                                cvt1[:], ev[:, 1:T + 1], w(1), cvt0[:], ALU.mult, ALU.add))
                            s_cv.wait(v)
                            s_cv.inc(v.scalar_tensor_tensor(
                                cvt0[:], ev[:, 2:T + 2], w(2), cvt1[:], ALU.mult, ALU.add))
                            s_cv.wait(v)
                            s_cv.inc(v.scalar_tensor_tensor(
                                cvout[:], ev[:, 3:T + 3], w(3), cvt0[:], ALU.mult, ALU.add))

                        @block.sync
                        def _(sync, oc=oc):
                            s_cv.wait(sync)
                            s_sp1.dinc(sync.dma_start(
                                out=d_xbc[:, :, oc, :].transpose([1, 0, 2]),
                                in_=cvout[:].rearrange("p (c t) -> p c t", c=NC, t=CH)))

                @block.vector
                def _(v):
                    s_ev1.wait(v)
                    s_cv.inc(v.tensor_scalar_mul(ndt[:, 0:T], ndt[:, 0:T], -1.0))

            with ExitStack() as pa:
                ea = pa.enter_context
                sba = lambda n, s, d_: ea(nc.sbuf_tensor(n, list(s), d_))
                xb_a = sba("xb_a", (128, 2 * 32 * CH), BF16)
                xdtT_a = sba("xdtT_a", (128, D), BF16)
                bsdT_a = sba("bsdT_a", (128, D), BF16)
                cumT_a = sba("cumT_a", (32, 128), F32)
                sdT_a = sba("sdT_a", (32, 128), F32)
                sdtm_a = sba("sdtm_a", (128, 32), F32)
                cdc_a = sba("cdc_a", (32, 1), F32)
                rsel_a = sba("rsel_a", (32, 16), F32)
                cds_a = sba("cds_a", (128, 16), F32)
                gath_a = sba("gath_a", (128, 1040), F32)
                tt_a = sba("tt_a", (128, 1024), F32)
                ccst_a = sba("ccst_a", (128, 16), F32)
                cdcum_a = sba("cdcum_a", (128, 16), F32)
                ps_atr = [ea(nc.psum_tensor(f"ps_atr{i}", [128, 256], BF16))
                          for i in range(2)]
                ps_as = [ea(nc.psum_tensor(f"ps_as{i}", [128, 64], F32))
                         for i in range(2)]
                ps_asd = ea(nc.psum_tensor("ps_asd", [128, 48], F32))

                s_al = sem("s_al")
                s_apc = sem("s_apc")
                s_adv = sem("s_adv")
                s_aac = sem("s_aac")
                s_asp = sem("s_asp")
                s_cc = sem("s_cc")

                for c in range(NC):
                    @block.scalar
                    def _(s, c=c):
                        s_sp1.wait(s)
                        s_apc.wait(s)
                        s_al.dinc(s.dma_start(
                            out=xb_a[:, (c % 2) * 32 * CH:(c % 2 + 1) * 32 * CH],
                            in_=d_xbc[c][:, 0:32, :]
                            .rearrange("p b t -> p (b t)")))

                    @block.sync
                    def _(sync, c=c):
                        s_adv.wait(sync)
                        s_asp.dinc(sync.dma_start(
                            out=d_pvl[c],
                            in_=prev_ab[:, (c % 2) * 1024:(c % 2 + 1) * 1024]))

                    @block.vector
                    def _(v, c=c):
                        s_cv.wait(v)
                        s_ini.wait(v, INI_N)
                        s_adv.inc(v.tensor_tensor_scan(
                            cumT_a[:], ndt[:, c * CH:(c + 1) * CH], zeros32[:],
                            0.0, ALU.add, ALU.add))

                    @block.scalar
                    def _(s, c=c):
                        s_adv.wait(s)
                        s_aac.inc(s.activation(
                            out=sdT_a[:], in_=cumT_a[:], func=AF.Exp,
                            bias=cumT_a[:, CH - 1:CH], scale=cst_s[0:32, 2:3]))
                        s_aac.inc(s.activation(
                            out=cdc_a[:], in_=cumT_a[:, CH - 1:CH], func=AF.Exp))

                    @block.vector
                    def _(v, c=c):
                        s_aac.wait(v)
                        s_adv.inc(v.tensor_scalar(
                            rsel_a[:], selcd_s[:], cdc_a[:], None, ALU.mult))

                    @block.tensor
                    def _(t, c=c):
                        s_aac.wait(t)
                        s_adv.wait(t)
                        s_cl.wait(t)
                        s_apc.inc(t.transpose(ps_asd[:, 0:32], sdT_a[:], identf_s[0:32, 0:32]))
                        s_apc.inc(t.matmul(ps_asd[:, 32:48], halfsel_s[:], rsel_a[:],
                                           start=True, stop=True))

                    @block.vector
                    def _(v, c=c):
                        s_apc.wait(v)
                        s_adv.inc(v.tensor_copy(sdtm_a[:], ps_asd[:, 0:32]))
                        s_adv.inc(v.tensor_copy(cds_a[:], ps_asd[:, 32:48]))
                        s_adv.inc(v.tensor_copy(
                            cdpre_s[:, c * 16:(c + 1) * 16], cdp_s[:]))

                    for grp in range(8):
                        @block.tensor
                        def _(t, c=c, grp=grp):
                            s_al.wait(t)
                            s_adv.wait(t)
                            xb = xb_a[:, (c % 2) * 32 * CH:]
                            for k in range(4):
                                i = grp * 4 + k
                                s_apc.inc(t.transpose(
                                    ps_atr[k // 2][:, (k % 2) * 128:(k % 2) * 128 + 128],
                                    xb[:, i * CH:(i + 1) * CH], identbf_s[:]))

                        @block.vector
                        def _(v, c=c, grp=grp):
                            s_apc.wait(v)
                            for k in range(4):
                                i = grp * 4 + k
                                src = ps_atr[k // 2][:, (k % 2) * 128:(k % 2) * 128 + 128]
                                if grp < 4:
                                    s_adv.inc(v.tensor_copy(
                                        xdtT_a[:, (i % 16) * 128:(i % 16) * 128 + 128], src))
                                else:
                                    ii = i - 16
                                    sd3 = sdtm_a[:, 2 * ii:2 * ii + 2].unsqueeze(2) \
                                        .broadcast_to([128, 2, 64])
                                    s_adv.inc(v.tensor_tensor(
                                        bsdT_a[:, ii * 128:(ii + 1) * 128].rearrange(
                                            "p (a b) -> p a b", a=2, b=64),
                                        src.rearrange("p (a b) -> p a b", a=2, b=64),
                                        sd3, ALU.mult))

                    @block.sync
                    def _(sync, c=c):
                        s_adv.wait(sync)
                        s_asp.dinc(sync.dma_start(out=d_xdtT[c], in_=xdtT_a[:]))

                    for j in range(16):
                        @block.tensor
                        def _(t, c=c, j=j):
                            s_adv.wait(t)
                            for hh in range(2):
                                h = 2 * j + hh
                                s_apc.inc(t.matmul(
                                    ps_as[j % 2][0:64, :] if hh == 0 else ps_as[j % 2][64:128, :],
                                    bsdT_a[:, h * 64:(h + 1) * 64],
                                    xdtT_a[:, h * 64:(h + 1) * 64],
                                    start=True, stop=True,
                                    tile_position=(0, 0) if hh == 0 else (0, 64)))

                        @block.vector
                        def _(v, c=c, j=j):
                            s_apc.wait(v)
                            s_asp.wait(v)
                            src_b = prev_ab[:, (c % 2) * 1024:]
                            dst_b = prev_ab[:, ((c + 1) % 2) * 1024:]
                            s_adv.inc(v.scalar_tensor_tensor(
                                dst_b[:, j * 64:(j + 1) * 64],
                                src_b[:, j * 64:(j + 1) * 64],
                                cds_a[:, j:j + 1], ps_as[j % 2][:],
                                ALU.mult, ALU.add))

                    @block.vector
                    def _(v, c=c):
                        s_adv.wait(v)
                        s_adv.inc(v.tensor_tensor(cdp_s[:], cdp_s[:], cds_a[:], ALU.mult))

                @block.sync
                def _(sync):
                    s_adv.wait(sync)
                    s_cc.dinc(sync.dma_start(
                        out=d_ccin[:, 0:1024],
                        in_=prev_ab[:, (NC % 2) * 1024:(NC % 2) * 1024 + 1024]))
                    s_cc.dinc(sync.dma_start(out=d_ccin[:, 1024:1040], in_=cdp_s[:]))

                @block.gpsimd
                def _(g):
                    s_cc.wait(g)
                    s_clg.wait(g)
                    s_ccw.wait(g)
                    s_cc.inc(g.collective_compute(
                        "AllGather", ALU.bypass,
                        ins=[d_ccin[:]], outs=[d_ccout[:]],
                        replica_groups=[list(range(n_cores))]))

                @block.vector
                def _(v):
                    s_cl.wait(v)
                    s_ini.wait(v, INI_N)
                    s_adv.inc(v.memset(cdcum_a[:], 1.0))

                for j in range(n_cores - 2, -1, -1):
                    @block.scalar
                    def _(s, j=j):
                        s_cc.wait(s)
                        s_adv.wait(s)
                        s_cc.dinc(s.dma_start(
                            out=gath_a[:], in_=d_ccout[128 * j:128 * (j + 1)]))

                    @block.vector
                    def _(v, j=j):
                        s_cc.wait(v)
                        cd3 = cdcum_a[:].unsqueeze(2).broadcast_to([128, 16, 64])
                        s_adv.inc(v.tensor_tensor(
                            tt_a[:].rearrange("p (a b) -> p a b", a=16, b=64),
                            gath_a[:, 0:1024].rearrange("p (a b) -> p a b", a=16, b=64),
                            cd3, ALU.mult))
                        s_adv.wait(v)
                        s_adv.inc(v.scalar_tensor_tensor(
                            prefix_s[:], tt_a[:], pmask_s[:, j:j + 1], prefix_s[:],
                            ALU.mult, ALU.add))
                        s_adv.inc(v.tensor_scalar(
                            ccst_a[:], gath_a[:, 1024:1040], -1.0, pmask_s[:, j:j + 1],
                            ALU.add, ALU.mult))
                        s_adv.wait(v)
                        s_adv.inc(v.scalar_tensor_tensor(
                            cdcum_a[:], ccst_a[:], 1.0, cdcum_a[:], ALU.add, ALU.mult))

                if dbg:
                    @block.sync
                    def _(sync):
                        s_adv.wait(sync)
                        s_cv.wait(sync)
                        s_ev1.wait(sync)
                        s_cc.dinc(sync.dma_start(out=dbg_t["pref_dbg"], in_=prefix_s[:]))
                        s_cc.dinc(sync.dma_start(out=dbg_t["ndt_dbg"], in_=ndt[:, 0:T]))
                        s_cc.dinc(sync.dma_start(out=dbg_t["siluz_dbg"], in_=sz_s[:]))
                PREF_ADV = s_adv.n

            with ExitStack() as pb:
                eb = pb.enter_context
                sbb = lambda n, s, d_: eb(nc.sbuf_tensor(n, list(s), d_))
                bc_b = sbb("bc_b", (128, 2 * 32 * CH), BF16)
                xdt_b = sbb("xdt_b", (128, 2 * D), BF16)
                pvl_b = sbb("pvl_b", (128, 2 * 1024), F32)
                cumT_b = sbb("cumT_b", (32, 128), F32)
                cum_tm = sbb("cum_tm", (128, 32), F32)
                ngc_tm = sbb("ngc_tm", (128, 32), F32)
                ec_tm = sbb("ec_tm", (128, 32), F32)
                m1m_b = sbb("m1m_b", (128, 2 * 256), F32)
                dt_b = sbb("dt_b", (128, 2 * 256), BF16)
                mt_b = sbb("mt_b", (128, 2 * 256), BF16)
                prevc_b = sbb("prevc_b", (128, 1024), F32)
                prevbf_b = sbb("prevbf_b", (128, 1024), BF16)
                ts_b = sbb("ts_b", (128, 2 * 128), F32)
                ysb_b = sbb("ysb_b", (128, D), BF16)
                yz_b = sbb("yz_b", (128, D), BF16)
                ps_px = [eb(nc.psum_tensor(f"ps_px{i}", [128, 512], F32))
                         for i in range(2)]
                ps_py = [eb(nc.psum_tensor(f"ps_py{i}", [128, 320], F32))
                         for i in range(2)]
                ps_btr = [eb(nc.psum_tensor(f"ps_btr{i}", [128, 256], BF16))
                          for i in range(2)]
                ps_ms = eb(nc.psum_tensor("ps_ms", [128, 32], F32))

                s_bl = sem("s_bl")
                s_bpc = sem("s_bpc")
                s_bdv = sem("s_bdv")
                s_bac = sem("s_bac")
                s_bst = sem("s_bst")

                for c in range(NC):
                    @block.sync
                    def _(sync, c=c):
                        s_asp.wait(sync, 32 * NC)
                        if c == 0:
                            s_adv.wait(sync, PREF_ADV)
                            s_apc.wait(sync)
                            s_aac.wait(sync)
                            s_cc.wait(sync)
                        s_bpc.wait(sync)
                        s_bl.dinc(sync.dma_start(
                            out=bc_b[:, (c % 2) * 32 * CH:(c % 2 + 1) * 32 * CH],
                            in_=d_xbc[c][:, 16:48, :]
                            .rearrange("p b t -> p (b t)")))
                        s_bl.dinc(sync.dma_start(
                            out=xdt_b[:, (c % 2) * D:(c % 2 + 1) * D], in_=d_xdtT[c]))
                        s_bl.dinc(sync.dma_start(
                            out=pvl_b[:, (c % 2) * 1024:(c % 2 + 1) * 1024], in_=d_pvl[c]))

                    @block.vector
                    def _(v, c=c):
                        s_cv.wait(v)
                        s_adv.wait(v, PREF_ADV)
                        s_ini.wait(v, INI_N)
                        s_bdv.inc(v.tensor_tensor_scan(
                            cumT_b[:], ndt[:, c * CH:(c + 1) * CH], zeros32[:],
                            0.0, ALU.add, ALU.add))
                        s_bl.wait(v)
                        cd3 = cdpre_s[:, c * 16:(c + 1) * 16].unsqueeze(2) \
                            .broadcast_to([128, 16, 64])
                        s_bdv.inc(v.tensor_tensor(
                            prevc_b[:].rearrange("p (a b) -> p a b", a=16, b=64),
                            prefix_s[:].rearrange("p (a b) -> p a b", a=16, b=64),
                            cd3, ALU.mult))
                        s_bdv.wait(v)
                        s_bdv.inc(v.tensor_add(
                            prevc_b[:], prevc_b[:],
                            pvl_b[:, (c % 2) * 1024:(c % 2) * 1024 + 1024]))
                        s_bdv.wait(v)
                        s_bdv.inc(v.tensor_copy(prevbf_b[:], prevc_b[:]))

                    @block.tensor
                    def _(t, c=c):
                        s_bdv.wait(t)
                        s_bac.wait(t)
                        s_cl.wait(t)
                        s_bpc.inc(t.transpose(ps_ms[:, 0:32], cumT_b[:], identf_s[0:32, 0:32]))

                    @block.vector
                    def _(v, c=c):
                        s_bpc.wait(v)
                        s_bdv.inc(v.tensor_scalar_mul(ngc_tm[:], ps_ms[:, 0:32], -1.0))
                        s_bdv.inc(v.tensor_copy(cum_tm[:], ps_ms[:, 0:32]))

                    @block.scalar
                    def _(s, c=c):
                        s_bdv.wait(s)
                        s_bac.inc(s.activation(out=ec_tm[:], in_=cum_tm[:], func=AF.Exp))

                    xdtT = xdt_b[:, (c % 2) * D:]
                    bmk = {}

                    def pair_m1m(v, c, j):
                        cum2 = cum_tm[:, 2 * j:2 * j + 2].unsqueeze(2) \
                            .broadcast_to([128, 2, 128])
                        s_bdv.inc(v.tensor_tensor(
                            m1m_b[:, (j % 2) * 256:(j % 2) * 256 + 256].rearrange(
                                "p (a b) -> p a b", a=2, b=128),
                            maskt2_s[:].rearrange("p (a b) -> p a b", a=2, b=128),
                            cum2, ALU.add))
                        bmk[("m1m", c, j)] = s_bdv.n

                    @block.vector
                    def _(v, c=c):
                        s_bdv.wait(v)
                        pair_m1m(v, c, 0)
                        pair_m1m(v, c, 1)

                    for j in range(16):
                        b = j % 2

                        @block.tensor
                        def _(t, c=c, j=j, b=b):
                            t.wait_ge(s_bdv.h, bmk[("m1m", c, j)])
                            t.wait_ge(s_bdv.h, bmk.get(("comb", c, j - 2),
                                                       bmk.get(("combprev", c), 0)))
                            for hh, pp in ((0, ps_px[b]), (1, ps_py[b])):
                                s_bpc.inc(t.transpose(
                                    pp[:, 0:128],
                                    m1m_b[:, b * 256 + hh * 128:b * 256 + hh * 128 + 128],
                                    identf_s[:]))
                            bmk[("tr", c, j)] = s_bpc.n

                        @block.scalar
                        def _(s, c=c, j=j, b=b):
                            s.wait_ge(s_bpc.h, bmk[("tr", c, j)])
                            for hh, pp in ((0, ps_px[b]), (1, ps_py[b])):
                                h = 2 * j + hh
                                s_bac.inc(s.activation(
                                    out=dt_b[:, b * 256 + hh * 128:b * 256 + hh * 128 + 128],
                                    in_=pp[:, 0:128],
                                    func=AF.Exp, bias=ngc_tm[:, h:h + 1]))
                            bmk[("dt", c, j)] = s_bac.n

                        @block.tensor
                        def _(t, c=c, j=j, b=b):
                            bc = bc_b[:, (c % 2) * 32 * CH:]
                            if j >= 1:
                                jj = j - 1
                                bb = jj % 2
                                t.wait_ge(s_bdv.h, bmk[("mt", c, jj)])
                                for hh, pp in ((0, ps_px[bb]), (1, ps_py[bb])):
                                    h = 2 * jj + hh
                                    s_bpc.inc(t.matmul(
                                        ps_px[bb][:, 320 + hh * 64:320 + hh * 64 + 64],
                                        mt_b[:, bb * 256 + hh * 128:bb * 256 + hh * 128 + 128],
                                        xdtT[:, h * 64:(h + 1) * 64],
                                        start=True, stop=True))
                                    ct = bc[64 * hh:64 * hh + 64,
                                            (16 + jj) * CH:(17 + jj) * CH]
                                    pv = prevbf_b[64 * hh:64 * hh + 64,
                                                  jj * 64:(jj + 1) * 64]
                                    s_bpc.inc(t.matmul(
                                        pp[:, 256:320],
                                        ct, pv, start=True, stop=True,
                                        tile_position=(64 * hh, 0)))
                                bmk[("ymm", c, jj)] = s_bpc.n
                            t.wait_ge(s_bac.h, bmk[("dt", c, j)])
                            s_bl.wait(t)
                            for hh, pp in ((0, ps_px[b]), (1, ps_py[b])):
                                bt = bc[64 * hh:64 * hh + 64, j * CH:(j + 1) * CH]
                                ct = bc[64 * hh:64 * hh + 64,
                                        (16 + j) * CH:(17 + j) * CH]
                                s_bpc.inc(t.matmul(
                                    pp[:, 128:256],
                                    bt, ct, start=True, stop=True,
                                    tile_position=(64 * hh, 0)))
                            bmk[("cbt", c, j)] = s_bpc.n

                        @block.vector
                        def _(v, c=c, j=j, b=b):
                            if j >= 1:
                                jj = j - 1
                                bb = jj % 2
                                v.wait_ge(s_bpc.h, bmk[("ymm", c, jj)])
                                for hh, pp in ((0, ps_px[bb]), (1, ps_py[bb])):
                                    h = 2 * jj + hh
                                    s_bdv.inc(v.tensor_scalar(
                                        ts_b[:, bb * 128 + hh * 64:bb * 128 + hh * 64 + 64],
                                        pp[:, 256:320],
                                        ec_tm[:, h:h + 1], None, ALU.mult))
                                s_bdv.wait(v)
                                s_bdv.inc(v.tensor_add(
                                    ts_b[:, bb * 128:bb * 128 + 128],
                                    ts_b[:, bb * 128:bb * 128 + 128],
                                    ps_px[bb][:, 320:448]))
                                s_bdv.wait(v)
                                for hh in (0, 1):
                                    h = 2 * jj + hh
                                    s_bdv.inc(v.scalar_tensor_tensor(
                                        ysb_b[:, jj * 128 + hh * 64:jj * 128 + hh * 64 + 64],
                                        xdtT[:, jj * 128 + hh * 64:jj * 128 + hh * 64 + 64],
                                        dcol_s[:, h:h + 1],
                                        ts_b[:, bb * 128 + hh * 64:bb * 128 + hh * 64 + 64],
                                        ALU.mult, ALU.add))
                                bmk[("comb", c, jj)] = s_bdv.n
                            v.wait_ge(s_bpc.h, bmk[("cbt", c, j)])
                            for hh, pp in ((0, ps_px[b]), (1, ps_py[b])):
                                s_bdv.inc(v.tensor_tensor(
                                    mt_b[:, b * 256 + hh * 128:b * 256 + hh * 128 + 128],
                                    pp[:, 128:256],
                                    dt_b[:, b * 256 + hh * 128:b * 256 + hh * 128 + 128],
                                    ALU.mult))
                            bmk[("mt", c, j)] = s_bdv.n
                            if j + 2 < 16:
                                pair_m1m(v, c, j + 2)

                    @block.tensor
                    def _(t, c=c):
                        jj = 15
                        bb = jj % 2
                        bc = bc_b[:, (c % 2) * 32 * CH:]
                        t.wait_ge(s_bdv.h, bmk[("mt", c, jj)])
                        for hh, pp in ((0, ps_px[bb]), (1, ps_py[bb])):
                            h = 2 * jj + hh
                            s_bpc.inc(t.matmul(
                                ps_px[bb][:, 320 + hh * 64:320 + hh * 64 + 64],
                                mt_b[:, bb * 256 + hh * 128:bb * 256 + hh * 128 + 128],
                                xdtT[:, h * 64:(h + 1) * 64],
                                start=True, stop=True))
                            ct = bc[64 * hh:64 * hh + 64,
                                    (16 + jj) * CH:(17 + jj) * CH]
                            pv = prevbf_b[64 * hh:64 * hh + 64,
                                          jj * 64:(jj + 1) * 64]
                            s_bpc.inc(t.matmul(
                                pp[:, 256:320],
                                ct, pv, start=True, stop=True,
                                tile_position=(64 * hh, 0)))
                        bmk[("ymm", c, jj)] = s_bpc.n

                    @block.vector
                    def _(v, c=c):
                        jj = 15
                        bb = jj % 2
                        v.wait_ge(s_bpc.h, bmk[("ymm", c, jj)])
                        for hh, pp in ((0, ps_px[bb]), (1, ps_py[bb])):
                            h = 2 * jj + hh
                            s_bdv.inc(v.tensor_scalar(
                                ts_b[:, bb * 128 + hh * 64:bb * 128 + hh * 64 + 64],
                                pp[:, 256:320],
                                ec_tm[:, h:h + 1], None, ALU.mult))
                        s_bdv.wait(v)
                        s_bdv.inc(v.tensor_add(
                            ts_b[:, bb * 128:bb * 128 + 128],
                            ts_b[:, bb * 128:bb * 128 + 128],
                            ps_px[bb][:, 320:448]))
                        s_bdv.wait(v)
                        for hh in (0, 1):
                            h = 2 * jj + hh
                            s_bdv.inc(v.scalar_tensor_tensor(
                                ysb_b[:, jj * 128 + hh * 64:jj * 128 + hh * 64 + 64],
                                xdtT[:, jj * 128 + hh * 64:jj * 128 + hh * 64 + 64],
                                dcol_s[:, h:h + 1],
                                ts_b[:, bb * 128 + hh * 64:bb * 128 + hh * 64 + 64],
                                ALU.mult, ALU.add))
                        bmk[("comb", c, jj)] = s_bdv.n
                        bmk[("combprev", c + 1)] = s_bdv.n

                    for grp in range(8):
                        @block.tensor
                        def _(t, c=c, grp=grp):
                            s_bdv.wait(t)
                            for k in range(2):
                                i = grp * 2 + k
                                s_bpc.inc(t.transpose(
                                    ps_btr[k][:, 0:128],
                                    ysb_b[:, i * 128:(i + 1) * 128], identbf_s[:]))

                        @block.vector
                        def _(v, c=c, grp=grp):
                            s_bpc.wait(v)
                            s_ev1.wait(v)
                            s_bst.wait(v)
                            for k in range(2):
                                i = grp * 2 + k
                                s_bdv.inc(v.tensor_tensor(
                                    yz_b[:, i * 128:(i + 1) * 128],
                                    ps_btr[k][:, 0:128],
                                    sz_s[:, i * T + c * CH:i * T + c * CH + CH],
                                    ALU.mult))

                    @block.sync
                    def _(sync, c=c):
                        s_bdv.wait(sync)
                        s_bst.dinc(sync.dma_start(out=d_yz[c], in_=yz_b[:]))

                SSDB_DV = s_bdv.n
                SSDB_PC = s_bpc.n
                SSDB_AC = s_bac.n
                SSDB_ST = s_bst.n


        with ExitStack() as pc:
            ec_ = pc.enter_context
            sbc = lambda n, s, d_: ec_(nc.sbuf_tensor(n, list(s), d_))
            wo_c = sbc("wo_c", (128, HC * D), BF16)
            yz_c = sbc("yz_c", (128, 2 * D), BF16)
            u_c = sbc("u_c", (128, 2 * D), F32)
            h1bf_c = sbc("h1bf_c", (128, 2 * D), BF16)
            sexp_c = sbc("sexp_c", (128, E), F32)
            m8_c = sbc("m8_c", (128, 8), F32)
            st_c = sbc("st_c", (128, 32), F32)
            ps_wo = [ec_(nc.psum_tensor(f"ps_wo{i}", [128, 512], F32))
                     for i in range(2)]
            ps_ctr = [ec_(nc.psum_tensor(f"ps_ctr{i}", [128, 128], BF16))
                      for i in range(2)]
            ps_lg = ec_(nc.psum_tensor("ps_lg", [128, 8], F32))

            cmk = {}
            s_cw = sem("s_cw")
            s_cyz = sem("s_cyz")
            s_cu = sem("s_cu")
            s_cpc = sem("s_cpc")
            s_cdv = sem("s_cdv")
            s_cac = sem("s_cac")
            s_cst = sem("s_cst")

            @block.gpsimd
            def _(g):
                s_bdv.wait(g, SSDB_DV)
                s_bpc.wait(g, SSDB_PC)
                s_bac.wait(g, SSDB_AC)
                s_bst.wait(g, SSDB_ST)
                s_cw.dinc(g.dma_start(
                    out=wo_c[:].rearrange("p (ic o) -> p ic o", ic=HC, o=D),
                    in_=i_wot[:].transpose([1, 0, 2])))

            def emit_trgrps(cc):
                for grp in range(8):
                    @block.tensor
                    def _(t, cc=cc, grp=grp):
                        t.wait_ge(s_cdv.h, cmk[("h1c", cc)])
                        if grp > 0:
                            t.wait_ge(s_cdv.h, cmk[("h1Tg", cc, grp - 1)])
                        for k in range(2):
                            i = grp * 2 + k
                            s_cpc.inc(t.transpose(
                                ps_ctr[k][:],
                                h1bf_c[:, (cc % 2) * D + i * 128:
                                       (cc % 2) * D + (i + 1) * 128],
                                identbf_s[:]))

                    @block.vector
                    def _(v, cc=cc, grp=grp):
                        s_cpc.wait(v)
                        for k in range(2):
                            i = grp * 2 + k
                            s_cdv.inc(v.tensor_copy(
                                h1T[:, i * T + cc * CH:i * T + cc * CH + CH],
                                ps_ctr[k][:]))
                        cmk[("h1Tg", cc, grp)] = s_cdv.n
                        if grp == 7:
                            cmk[("h1T", cc)] = s_cdv.n

            def emit_loggate(cc):
                pg = 16 * (cc % 2)

                @block.tensor
                def _(t, cc=cc):
                    t.wait_ge(s_cdv.h, cmk[("h1T", cc)])
                    t.wait_ge(s_cac.h, cmk.get(("sexp", cc - 1), 0))
                    t.wait_ge(s_cdv.h, cmk.get(("gmax", cc - 1), 0))
                    s_clg.wait(t)
                    for ic in range(HC):
                        s_cpc.inc(t.matmul(
                            ps_lg[:],
                            h1T[:, ic * T + cc * CH:ic * T + cc * CH + CH],
                            gw_s[:, ic * E:(ic + 1) * E],
                            start=(ic == 0), stop=(ic == HC - 1)))

                @block.vector
                def _(v, cc=cc, pg=pg):
                    s_cpc.wait(v)
                    s_cdv.inc(v.reduce_max(st_c[:, pg + 13:pg + 14], ps_lg[:], AX.X))
                    cmk[("gmax", cc)] = s_cdv.n
                    s_cdv.wait(v)
                    s_cdv.inc(v.tensor_scalar_mul(
                        st_c[:, pg + 14:pg + 15], st_c[:, pg + 13:pg + 14], -1.0))

                @block.scalar
                def _(s, cc=cc, pg=pg):
                    s_cdv.wait(s)
                    s_cac.inc(s.activation(
                        out=sexp_c[:], in_=ps_lg[:], func=AF.Exp,
                        bias=st_c[:, pg + 14:pg + 15],
                        accum_out=st_c[:, pg + 15:pg + 16]))
                    cmk[("sexp", cc)] = s_cac.n

                @block.vector
                def _(v, cc=cc, pg=pg):
                    s_cac.wait(v)
                    s_cdv.inc(v.max(out=m8_c[:], in_=sexp_c[:]))
                    s_cdv.wait(v)
                    s_cdv.inc(v.tensor_add(
                        st_c[:, pg:pg + 1], m8_c[:, 0:1], m8_c[:, 1:2]))
                    s_cdv.wait(v)
                    s_cdv.inc(v.scalar_tensor_tensor(
                        st_c[:, pg + 1:pg + 2], st_c[:, pg + 15:pg + 16], 1.0e-20,
                        st_c[:, pg:pg + 1], ALU.mult, ALU.add))
                    s_cdv.wait(v)
                    s_cdv.inc(v.reciprocal(st_c[:, pg + 2:pg + 3],
                                           st_c[:, pg + 1:pg + 2]))
                    s_cdv.inc(v.tensor_scalar(
                        m8_c[:], sexp_c[:], m8_c[:, 1:2], None, ALU.is_ge))
                    s_cdv.wait(v)
                    s_cdv.inc(v.tensor_mul(sexp_c[:], sexp_c[:], m8_c[:]))
                    s_cdv.wait(v)
                    s_cdv.inc(v.tensor_scalar(
                        cw_s[:, cc * E:(cc + 1) * E], sexp_c[:],
                        st_c[:, pg + 2:pg + 3], None, ALU.mult))

            for c in range(NC):
                @block.sync
                def _(sync, c=c):
                    if c == 0:
                        s_bst.wait(sync, SSDB_ST)
                        s_bdv.wait(sync, SSDB_DV)
                        s_bac.wait(sync, SSDB_AC)
                        s_bpc.wait(sync, SSDB_PC)
                    sync.wait_ge(s_cpc.h, cmk.get(("mm", c - 2, 3), 0))
                    s_cyz.dinc(sync.dma_start(
                        out=yz_c[:, (c % 2) * D:(c % 2 + 1) * D], in_=d_yz[c]))
                    sync.wait_ge(s_cdv.h, cmk.get(("ev", c - 2, 3), 0))
                    s_cu.dinc(sync.dma_start(
                        out=u_c[:, (c % 2) * D:(c % 2 + 1) * D],
                        in_=i_utm[c * CH:(c + 1) * CH]))
                    cmk[("ul", c)] = s_cu.n

                po = 16 * (c % 2)
                for q in range(4):
                    @block.tensor
                    def _(t, c=c, q=q):
                        s_cw.wait(t)
                        t.wait_ge(s_cyz.h, 16 * (c + 1))
                        t.wait_ge(s_cdv.h, cmk.get(("ev",
                            c if q >= 2 else c - 1, (q - 2) % 4), 0))
                        yz = yz_c[:, (c % 2) * D:]
                        for ic in range(HC):
                            s_cpc.inc(t.matmul(
                                ps_wo[q % 2][:],
                                yz[:, ic * 128:(ic + 1) * 128],
                                wo_c[:, ic * D + q * 512:ic * D + q * 512 + 512],
                                start=(ic == 0), stop=(ic == HC - 1)))
                        cmk[("mm", c, q)] = s_cpc.n

                    @block.vector
                    def _(v, c=c, q=q, po=po):
                        v.wait_ge(s_cpc.h, cmk[("mm", c, q)])
                        v.wait_ge(s_cu.h, cmk[("ul", c)])
                        s_cdv.inc(v.scalar_tensor_tensor(
                            y_acc[:, c * D + q * 512:c * D + q * 512 + 512],
                            ps_wo[q % 2][:], 1.0,
                            u_c[:, (c % 2) * D + q * 512:(c % 2) * D + q * 512 + 512],
                            ALU.mult, ALU.add,
                            accum_out=st_c[:, po + q:po + q + 1]))
                        cmk[("ev", c, q)] = s_cdv.n

                    if q == 1 and c >= 1:
                        emit_trgrps(c - 1)
                    if q == 2 and c >= 1:
                        emit_loggate(c - 1)

                @block.vector
                def _(v, c=c, po=po):
                    s_cdv.wait(v)
                    s_cdv.inc(v.reduce_sum(st_c[:, po + 4:po + 5],
                                           st_c[:, po:po + 4], AX.X))
                    s_cdv.wait(v)
                    s_cdv.inc(v.tensor_scalar_mul(
                        st_c[:, po + 5:po + 6], st_c[:, po + 4:po + 5], 1.0 / D))

                @block.scalar
                def _(s, c=c, po=po):
                    s_cdv.wait(s)
                    s_cac.inc(s.activation(
                        out=h1bf_c[:, (c % 2) * D:(c % 2 + 1) * D],
                        in_=y_acc[:, c * D:(c + 1) * D], func=AF.Square,
                        accum_out=st_c[:, po + 6:po + 7]))

                @block.vector
                def _(v, c=c, po=po):
                    s_cac.wait(v)
                    s_cdv.inc(v.tensor_scalar_mul(
                        st_c[:, po + 7:po + 8], st_c[:, po + 6:po + 7], 1.0 / D))
                    s_cdv.inc(v.tensor_mul(
                        st_c[:, po + 8:po + 9], st_c[:, po + 5:po + 6],
                        st_c[:, po + 5:po + 6]))
                    s_cdv.wait(v)
                    s_cdv.inc(v.tensor_sub(
                        st_c[:, po + 9:po + 10], st_c[:, po + 7:po + 8],
                        st_c[:, po + 8:po + 9]))

                @block.scalar
                def _(s, c=c, po=po):
                    s_cdv.wait(s)
                    s_cac.inc(s.activation(out=st_c[:, po + 10:po + 11],
                                           in_=st_c[:, po + 9:po + 10],
                                           func=AF.Ln, bias=cst_s[:, 0:1]))
                    s_cac.wait(s)
                    s_cac.inc(s.activation(out=st_c[:, po + 11:po + 12],
                                           in_=st_c[:, po + 10:po + 11],
                                           func=AF.Exp, scale=cst_s[:, 1:2]))

                @block.vector
                def _(v, c=c, po=po):
                    s_cac.wait(v)
                    s_cdv.inc(v.scalar_tensor_tensor(
                        st_c[:, po + 12:po + 13], st_c[:, po + 5:po + 6], -1.0,
                        st_c[:, po + 11:po + 12], ALU.mult, ALU.mult))

                @block.scalar
                def _(s, c=c, po=po):
                    s_cdv.wait(s)
                    s_cac.inc(s.activation(
                        out=y_acc[:, c * D:(c + 1) * D],
                        in_=y_acc[:, c * D:(c + 1) * D], func=AF.Identity,
                        scale=st_c[:, po + 11:po + 12],
                        bias=st_c[:, po + 12:po + 13]))
                    cmk[("rln", c)] = s_cac.n

                @block.vector
                def _(v, c=c):
                    s_cac.wait(v)
                    s_cdv.inc(v.tensor_copy(
                        h1bf_c[:, (c % 2) * D:(c % 2 + 1) * D],
                        y_acc[:, c * D:(c + 1) * D]))
                    cmk[("h1c", c)] = s_cdv.n

                if dbg:
                    @block.sync
                    def _(sync, c=c):
                        s_cdv.wait(sync)
                        s_cst.dinc(sync.dma_start(
                            out=dbg_t["r_dbg"][c], in_=y_acc[:, c * D:(c + 1) * D]))

            emit_trgrps(NC - 1)
            emit_loggate(NC - 1)

            if dbg:
                @block.sync
                def _(sync):
                    s_cdv.wait(sync)
                    s_cac.wait(sync)
                    s_cst.dinc(sync.dma_start(out=dbg_t["h1_dbg"], in_=y_acc[:]))
                    s_cst.dinc(sync.dma_start(out=dbg_t["cw_dbg"], in_=cw_s[:]))

            SSDC_DV = s_cdv.n
            SSDC_PC = s_cpc.n
            SSDC_AC = s_cac.n
            SSDC_ST = s_cst.n

        with ExitStack() as pf:
            ef = pf.enter_context
            sbf = lambda n, s, d_: ef(nc.sbuf_tensor(n, list(s), d_))
            wg_f = sbf("wg_f", (128, 2 * D), BF16)
            wu_f = sbf("wu_f", (128, 2 * D), BF16)
            wd_f = sbf("wd_f", (128, FC * D), BF16)
            gu_f = sbf("gu_f", (128, 2 * T), BF16)
            gt_f = sbf("gt_f", (128, 2 * FC * CH), BF16)
            sg_f = sbf("sg_f", (128, 1024), F32)
            out_f = sbf("out_f", (128, D), F32)
            st_f = sbf("st_f", (128, 16), F32)
            ps_g = ef(nc.psum_tensor("ps_g", [128, 1024], F32))
            ps_u = ef(nc.psum_tensor("ps_u", [128, 1024], F32))
            ps_d = ef(nc.psum_tensor("ps_d", [128, 1024], F32))
            ps_d2 = ef(nc.psum_tensor("ps_d2", [128, 1024], F32))

            s_fw2 = [sem("s_fwA"), sem("s_fwB")]
            s_fd = sem("s_fd")
            s_fpc = sem("s_fpc")
            s_fdv = sem("s_fdv")
            s_fac = sem("s_fac")
            s_fsp2 = [sem("s_fspA"), sem("s_fspB")]
            s_fgl2 = [sem("s_fglA"), sem("s_fglB")]
            s_out = sem("s_out")

            EXS = [("shA", None), ("shB", None)] + [("e", e) for e in range(E)]

            def wgwu_src(xi, fc):
                nm, e = EXS[xi]
                if nm == "shA":
                    return i_swg[fc], i_swu[fc]
                if nm == "shB":
                    return i_swg[FC + fc], i_swu[FC + fc]
                return i_wg[e, fc], i_wu[e, fc]

            @block.gpsimd
            def _(g):
                s_cdv.wait(g, SSDC_DV)
                s_cpc.wait(g, SSDC_PC)
                s_cac.wait(g, SSDC_AC)
                s_cst.wait(g, SSDC_ST)
                for k in (0, 1):
                    sg_, su_ = wgwu_src(0, k)
                    s_fw2[k % 2].dinc(g.dma_start(
                        out=wg_f[:, (k % 2) * D:(k % 2 + 1) * D]
                        .rearrange("p (hc f) -> p hc f", hc=HC, f=128), in_=sg_))
                    s_fw2[k % 2].dinc(g.dma_start(
                        out=wu_f[:, (k % 2) * D:(k % 2 + 1) * D]
                        .rearrange("p (hc f) -> p hc f", hc=HC, f=128), in_=su_))

            NEX = len(EXS)
            TNL = [(i * 512, min(512, T - i * 512)) for i in range((T + 511) // 512)]
            mk = {}
            devac_mk = []

            def emit_unit(xi, fc):
                lin = xi * FC + fc

                @block.tensor
                def _(t, xi=xi, fc=fc, lin=lin):
                    t.wait_ge(s_fw2[lin % 2].h, 32 * (lin // 2 + 1))
                    if lin == 0:
                        s_cdv.wait(t, SSDC_DV)
                        s_cpc.wait(t, SSDC_PC)
                    for tn, (ts_, w_) in enumerate(TNL):
                        t.wait_ge(s_fac.h, mk.get(("silu", lin - 1, tn), 0))
                        t.wait_ge(s_fdv.h, mk.get(("gmul", lin - 1, tn), 0))
                        for hc in range(HC):
                            s_fpc.inc(t.matmul(
                                ps_g[:, ts_:ts_ + w_],
                                wg_f[:, (lin % 2) * D + hc * 128:
                                     (lin % 2) * D + hc * 128 + 128],
                                h1T[:, hc * T + ts_:hc * T + ts_ + w_],
                                start=(hc == 0), stop=(hc == HC - 1)))
                        mk[("mmg", lin, tn)] = s_fpc.n
                        for hc in range(HC):
                            s_fpc.inc(t.matmul(
                                ps_u[:, ts_:ts_ + w_],
                                wu_f[:, (lin % 2) * D + hc * 128:
                                     (lin % 2) * D + hc * 128 + 128],
                                h1T[:, hc * T + ts_:hc * T + ts_ + w_],
                                start=(hc == 0), stop=(hc == HC - 1)))
                        mk[("mmu", lin, tn)] = s_fpc.n

                if lin + 2 < NEX * FC:
                    @block.gpsimd
                    def _(g, lin=lin):
                        s_fpc.wait(g)
                        nxi, nfc = divmod(lin + 2, FC)
                        sg_, su_ = wgwu_src(nxi, nfc)
                        s_fw2[lin % 2].dinc(g.dma_start(
                            out=wg_f[:, (lin % 2) * D:(lin % 2 + 1) * D]
                            .rearrange("p (hc f) -> p hc f", hc=HC, f=128), in_=sg_))
                        s_fw2[lin % 2].dinc(g.dma_start(
                            out=wu_f[:, (lin % 2) * D:(lin % 2 + 1) * D]
                            .rearrange("p (hc f) -> p hc f", hc=HC, f=128), in_=su_))

                for tn, (ts_, w_) in enumerate(TNL):
                    @block.scalar
                    def _(s, lin=lin, tn=tn, ts_=ts_, w_=w_):
                        s.wait_ge(s_fpc.h, mk[("mmg", lin, tn)])
                        s.wait_ge(s_fdv.h, mk.get(("gmul", lin - 1, tn), 0))
                        s_fac.inc(s.activation(
                            out=sg_f[:, ts_:ts_ + w_],
                            in_=ps_g[:, ts_:ts_ + w_], func=AF.Silu))
                        mk[("silu", lin, tn)] = s_fac.n

                    @block.vector
                    def _(v, lin=lin, tn=tn, ts_=ts_, w_=w_):
                        v.wait_ge(s_fac.h, mk[("silu", lin, tn)])
                        v.wait_ge(s_fpc.h, mk[("mmu", lin, tn)])
                        sp = mk.get(("spill", lin - 2), (0, 0))
                        v.wait_ge(s_fsp2[lin % 2].h, sp[lin % 2])
                        s_fdv.inc(v.tensor_tensor(
                            gu_f[:, (lin % 2) * T + ts_:(lin % 2) * T + ts_ + w_],
                            sg_f[:, ts_:ts_ + w_],
                            ps_u[:, ts_:ts_ + w_], ALU.mult))
                        mk[("gmul", lin, tn)] = s_fdv.n

                @block.sync
                def _(sync, xi=xi, fc=fc, lin=lin):
                    sync.wait_ge(s_fdv.h, mk[("gmul", lin, len(TNL) - 1)])
                    gtls = mk.get(("gtls", xi - 2), (0, 0))
                    sync.wait_ge(s_fgl2[0].h, gtls[0])
                    sync.wait_ge(s_fgl2[1].h, gtls[1])
                    s_fsp2[lin % 2].dinc(sync.dma_start(
                        out=d_g[xi % 2][:, :, fc, :].transpose([1, 0, 2]),
                        in_=gu_f[:, (lin % 2) * T:(lin % 2 + 1) * T]
                        .rearrange("p (c t) -> p c t", c=NC, t=CH)))
                    mk[("spill", lin)] = (s_fsp2[0].n, s_fsp2[1].n)
                if fc == FC - 1:
                    mk[("spills", xi)] = (s_fsp2[0].n, s_fsp2[1].n)

            def emit_down_tc(y, tc):
                nm, ex = EXS[y]
                dg = y * NC + tc

                @block.scalar
                def _(s, y=y, tc=tc, dg=dg):
                    s.wait_ge(s_fsp2[0].h, mk[("spills", y)][0])
                    s.wait_ge(s_fsp2[1].h, mk[("spills", y)][1])
                    s.wait_ge(s_fpc.h, mk.get(("down_mm", dg - 2), 0))
                    s_fgl2[dg % 2].dinc(s.dma_start(
                        out=gt_f[:, (tc % 2) * FC * CH:(tc % 2 + 1) * FC * CH],
                        in_=d_g[y % 2, tc]
                        .rearrange("p fc t -> p (fc t)")))
                    mk[("gtl", dg)] = s_fgl2[dg % 2].n
                    if tc == NC - 1:
                        mk[("gtls", y)] = (s_fgl2[0].n, s_fgl2[1].n)

                for half in range(2):
                    @block.tensor
                    def _(t, y=y, tc=tc, dg=dg, half=half):
                        t.wait_ge(s_fgl2[dg % 2].h, mk[("gtl", dg)])
                        t.wait_ge(s_fd.h, 16 * FC * (y + 1))
                        if len(devac_mk) >= 2:
                            t.wait_ge(s_fdv.h, devac_mk[-2])
                        gt = gt_f[:, (tc % 2) * FC * CH:]
                        pd = ps_d if half == 0 else ps_d2
                        for hn in (2 * half, 2 * half + 1):
                            for fc in range(FC):
                                s_fpc.inc(t.matmul(
                                    pd[:, (hn % 2) * 512:(hn % 2) * 512 + 512],
                                    gt[:, fc * CH:(fc + 1) * CH],
                                    wd_f[:, fc * D + hn * 512:fc * D + hn * 512 + 512],
                                    start=(fc == 0), stop=(fc == FC - 1)))
                            mk[("dmm", dg, hn)] = s_fpc.n
                        if half == 1:
                            mk[("down_mm", dg)] = s_fpc.n

                    @block.vector
                    def _(v, y=y, nm=nm, ex=ex, tc=tc, dg=dg, half=half):
                        pd = ps_d if half == 0 else ps_d2
                        for hn in (2 * half, 2 * half + 1):
                            v.wait_ge(s_fpc.h, mk[("dmm", dg, hn)])
                            dst = y_acc[:, tc * D + hn * 512:tc * D + hn * 512 + 512]
                            if nm in ("shA", "shB"):
                                s_fdv.inc(v.tensor_add(
                                    dst, dst,
                                    pd[:, (hn % 2) * 512:(hn % 2) * 512 + 512]))
                            else:
                                s_fdv.inc(v.scalar_tensor_tensor(
                                    dst,
                                    pd[:, (hn % 2) * 512:(hn % 2) * 512 + 512],
                                    cw_s[:, tc * E + ex:tc * E + ex + 1],
                                    dst, ALU.mult, ALU.add))
                        devac_mk.append(s_fdv.n)

            def emit_wd(y, grp_):
                @block.gpsimd
                def _(g, y=y, grp_=grp_):
                    g.wait_ge(s_fpc.h, mk.get(("down_mm", y * NC - 1), 0))
                    nm2, ex2 = EXS[y]
                    if nm2 == "shA":
                        src = i_swd
                    elif nm2 == "shB":
                        src = i_swd[FC:]
                    else:
                        src = i_wd[ex2]
                    for k in grp_:
                        s_fd.dinc(g.dma_start(
                            out=wd_f[:, k * D:(k + 1) * D], in_=src[k]))

            for xi in range(NEX):
                for fc in range(FC):
                    emit_unit(xi, fc)
                    if xi >= 1 and 3 <= fc < NC + 3:
                        emit_down_tc(xi - 1, fc - 3)
                    if fc == FC - 1:
                        emit_wd(xi, range(0, 4))
                    if xi >= 1 and fc == 0:
                        emit_wd(xi - 1, range(4, 8))
                    if xi >= 1 and fc == 1:
                        emit_wd(xi - 1, range(8, FC))
            emit_wd(NEX - 1, range(4, 8))
            emit_wd(NEX - 1, range(8, FC))
            for tc in range(NC):
                emit_down_tc(NEX - 1, tc)

            for tc in range(NC):
                @block.vector
                def _(v, tc=tc):
                    s_fdv.wait(v)
                    s_fdv.inc(v.reduce_sum(st_f[:, 0:1],
                                           y_acc[:, tc * D:(tc + 1) * D], AX.X))
                    s_fdv.wait(v)
                    s_fdv.inc(v.tensor_scalar_mul(st_f[:, 1:2], st_f[:, 0:1], 1.0 / D))

                @block.scalar
                def _(s, tc=tc):
                    s_fdv.wait(s)
                    s_fac.wait(s)
                    s_out.wait(s)
                    s_fac.inc(s.activation(out=out_f[:], in_=y_acc[:, tc * D:(tc + 1) * D],
                                           func=AF.Square, accum_out=st_f[:, 2:3]))

                @block.vector
                def _(v, tc=tc):
                    s_fac.wait(v)
                    s_fdv.inc(v.tensor_scalar_mul(st_f[:, 3:4], st_f[:, 2:3], 1.0 / D))
                    s_fdv.inc(v.tensor_mul(st_f[:, 4:5], st_f[:, 1:2], st_f[:, 1:2]))
                    s_fdv.wait(v)
                    s_fdv.inc(v.tensor_sub(st_f[:, 5:6], st_f[:, 3:4], st_f[:, 4:5]))

                @block.scalar
                def _(s, tc=tc):
                    s_fdv.wait(s)
                    s_fac.inc(s.activation(out=st_f[:, 6:7], in_=st_f[:, 5:6],
                                           func=AF.Ln, bias=cst_s[:, 0:1]))
                    s_fac.wait(s)
                    s_fac.inc(s.activation(out=st_f[:, 7:8], in_=st_f[:, 6:7],
                                           func=AF.Exp, scale=cst_s[:, 1:2]))

                @block.vector
                def _(v, tc=tc):
                    s_fac.wait(v)
                    s_fdv.inc(v.scalar_tensor_tensor(
                        st_f[:, 8:9], st_f[:, 1:2], -1.0, st_f[:, 7:8],
                        ALU.mult, ALU.mult))

                @block.scalar
                def _(s, tc=tc):
                    s_fdv.wait(s)
                    s_fac.inc(s.activation(
                        out=out_f[:], in_=y_acc[:, tc * D:(tc + 1) * D], func=AF.Identity,
                        scale=st_f[:, 7:8], bias=st_f[:, 8:9]))

                @block.sync
                def _(sync, tc=tc):
                    s_fac.wait(sync)
                    s_out.dinc(sync.dma_start(
                        out=o_out[tc * CH:(tc + 1) * CH, :], in_=out_f[:]))

            @block.sync
            def _(sync):
                s_out.wait(sync)
                if dbg:
                    s_bst.wait(sync)
                    s_cc.wait(sync)

    return nc


def _prep(inputs, n_cores, T):
    f32 = np.float32
    u = np.asarray(inputs["L0_attn_u"], f32)[0]
    W_in = np.asarray(inputs["L0_attn_W_in"], f32)
    conv_w = np.asarray(inputs["L0_attn_conv_w"], f32)
    conv_b = np.asarray(inputs["L0_attn_conv_b"], f32)
    Dv = np.asarray(inputs["L0_attn_D"], f32)
    z_bias = np.asarray(inputs["L0_attn_z_bias"], f32)
    W_out = np.asarray(inputs["L0_attn_W_out"], f32)
    gate_w = np.asarray(inputs["L0_ffn_gate_weight"], f32)
    w_gate = np.asarray(inputs["L0_ffn_w_gate"], f32)
    w_up = np.asarray(inputs["L0_ffn_w_up"], f32)
    w_down = np.asarray(inputs["L0_ffn_w_down"], f32)
    sw_gate = np.asarray(inputs["L0_ffn_sw_gate"], f32)
    sw_up = np.asarray(inputs["L0_ffn_sw_up"], f32)
    sw_down = np.asarray(inputs["L0_ffn_sw_down"], f32)

    Wp = np.zeros((NOC * 128, D), f32)
    Wp[:8224] = W_in
    w_in_t = np.ascontiguousarray(
        Wp.reshape(NOC, 128, HC, 128).transpose(0, 3, 2, 1))

    conv_w_c = np.ascontiguousarray(
        conv_w.reshape(48, 128, 4).transpose(1, 0, 2).reshape(128, 48 * 4))
    conv_b_c = np.ascontiguousarray(conv_b.reshape(48, 128).T)
    d_col = np.ascontiguousarray(np.tile(Dv[None, :], (128, 1)))
    zb_c = np.ascontiguousarray(z_bias.reshape(16, 128).T)
    w_out_t = np.ascontiguousarray(W_out.T).reshape(HC, 128, D)
    gw_t = np.ascontiguousarray(gate_w.T).reshape(HC, 128, E)
    w_gate_t = np.ascontiguousarray(
        w_gate.reshape(E, FC, 128, HC, 128).transpose(0, 1, 4, 3, 2))
    w_up_t = np.ascontiguousarray(
        w_up.reshape(E, FC, 128, HC, 128).transpose(0, 1, 4, 3, 2))
    w_down_t = np.ascontiguousarray(w_down.transpose(0, 2, 1)).reshape(E, FC, 128, D)
    sw_gate_t = np.ascontiguousarray(
        sw_gate.reshape(FSC, 128, HC, 128).transpose(0, 3, 2, 1))
    sw_up_t = np.ascontiguousarray(
        sw_up.reshape(FSC, 128, HC, 128).transpose(0, 3, 2, 1))
    sw_down_t = np.ascontiguousarray(sw_down.T).reshape(FSC, 128, D)

    shared = dict(w_in_t=w_in_t, conv_w_c=conv_w_c, conv_b_c=conv_b_c,
                  d_col=d_col, zb_c=zb_c, w_out_t=w_out_t, gw_t=gw_t,
                  w_gate_t=w_gate_t, w_up_t=w_up_t, w_down_t=w_down_t,
                  sw_gate_t=sw_gate_t, sw_up_t=sw_up_t, sw_down_t=sw_down_t)

    in_maps = []
    for c in range(n_cores):
        s = c * T
        seg = np.zeros((T + 3, D), f32)
        lo = max(0, s - 3)
        seg[3 - (s - lo):] = u[lo:s + T]
        u_fm = np.ascontiguousarray(seg.T).reshape(HC, 128, T + 3)
        u_tm = np.ascontiguousarray(u[s:s + T])
        pm = np.zeros((128, NCORES), f32)
        pm[:, :c] = 1.0
        m = dict(shared)
        m.update(u_fm=u_fm, u_tm=u_tm, pmask=pm)
        in_maps.append(m)
    return in_maps


_CACHE = {}


def kernel(**inputs):
    T = 8192 // NCORES
    key = (T, NCORES)
    if key not in _CACHE:
        _CACHE[key] = build(T, NCORES)
    nc = _CACHE[key]
    in_maps = _prep(inputs, NCORES, T)
    res = run_bass_kernel_spmd(nc, in_maps, core_ids=list(range(NCORES)))
    out = np.concatenate([r["out"] for r in res.results], axis=0)
    return out.reshape(1, 8192, D).astype(np.float32)
